# version 03
# baseline (speedup 1.0000x reference)
import numpy as np
import ml_dtypes
import concourse.bass as bass
import concourse.mybir as mybir
from concourse.bass_utils import run_bass_kernel_spmd

F32 = mybir.dt.float32
BF16 = mybir.dt.bfloat16
AF = mybir.ActivationFunctionType
ALU = mybir.AluOpType

L, H, NH = 8192, 1024, 4
DK = DV = 256
TS = 512
NSEG = L // TS
C = 128
NCH = TS // C
FT = 256
KT = H // 128

BF = ml_dtypes.bfloat16


class Bf:
    __slots__ = ("w", "r", "x")
    def __init__(s, x=False):
        s.w = -1
        s.r = []
        s.x = x


class Prog:
    def __init__(s):
        s.ops = []

    def op(s, eng, fn, r=(), w=(), dma=False):
        rr = [b for b in r if not b.x]
        ww = list(w) + [b for b in r if b.x]
        deps = set()
        for b in rr:
            if b.w >= 0:
                deps.add(b.w)
        for b in ww:
            if b.w >= 0:
                deps.add(b.w)
            deps.update(b.r)
        i = len(s.ops)
        s.ops.append((eng, fn, deps, 16 if dma else 1))
        for b in rr:
            b.r.append(i)
        for b in ww:
            b.w = i
            b.r = []
        return i


def build_nc(nseg=NSEG):
    L_ = nseg * TS
    nc = bass.Bass("TRN2", target_bir_lowering=False)
    P = Prog()
    op = P.op

    def din(name, shape, dt=F32):
        return nc.dram_tensor(name, shape, dt, kind="ExternalInput").ap()

    xT = din("xT", [H, L_])
    Wq = din("Wq", [H, H]); Wk = din("Wk", [H, H]); Wv = din("Wv", [H, H])
    Wb = din("Wb", [H, NH])
    Wf1 = din("Wf1", [H, 2 * H]); bf1 = din("bf1", [2 * H]); Wf2 = din("Wf2", [2 * H, 16]); bf2 = din("bf2", [16])
    Wo = din("Wo", [H, H]); rmsw = din("rmsw", [DV])
    fq = din("fq", [H, 4]); fk = din("fk", [H, 4]); fv = din("fv", [H, 4])
    fs = din("fs", [H, 7]); fl = din("fl", [H, 64])
    WfcT = din("WfcT", [H, 4 * H])
    WprojT = din("WprojT", [4 * H, H])
    ID_F = din("ID_F", [128, 128]); ID_B = din("ID_B", [128, 128], BF16)
    NEG_SL = din("NEG_SL", [128, 128]); CAUS_T = din("CAUS_T", [128, 128])
    ONEROW_F = din("ONEROW_F", [1, 128]); ONEROW_B = din("ONEROW_B", [1, 128], BF16)
    ONECOL_B = din("ONECOL_B", [128, 1], BF16)
    H4 = din("H4", [128, 16], BF16)
    IND4R = din("IND4R", [4, 512], BF16)
    outT = nc.dram_tensor("outT", [H, L_], F32, kind="ExternalOutput").ap()
    wf1_sc = nc.dram_tensor("wf1_sc", [128, 16384], BF16, kind="Internal").ap()
    wo_sc = nc.dram_tensor("wo_sc", [128, 8192], BF16, kind="Internal").ap()
    wfc_sc = nc.dram_tensor("wfc_sc", [128, 32768], BF16, kind="Internal").ap()
    wpj_sc = nc.dram_tensor("wpj_sc", [128, 32768], BF16, kind="Internal").ap()

    ctx = []

    def sb(name, shape, dt):
        t = nc.sbuf_tensor(name, shape, dt)
        ctx.append(t)
        return t.__enter__()

    def ps(name, shape, dt):
        t = nc.psum_tensor(name, shape, dt)
        ctx.append(t)
        return t.__enter__()

    wq_sb = sb("wq_sb", [128, 8192], BF16)
    wk_sb = sb("wk_sb", [128, 8192], BF16)
    wv_sb = sb("wv_sb", [128, 8192], BF16)
    wb_sb = sb("wb_sb", [128, 32], BF16)
    wf2_sb = sb("wf2_sb", [128, 256], BF16)
    bf1T = sb("bf1T", [128, 16], F32)
    bf2_sb = sb("bf2_sb", [16, 1], F32)
    rmsw_bc = sb("rmsw_bc", [128, DV], BF16)
    fq_sb = sb("fq_sb", [128, 32], F32)
    fk_sb = sb("fk_sb", [128, 32], F32)
    fv_sb = sb("fv_sb", [128, 32], F32)
    fs_sb = sb("fs_sb", [128, 56], F32)
    fl_sb = sb("fl_sb", [128, 512], F32)
    id_f = sb("id_f", [128, 128], F32)
    id_b = sb("id_b", [128, 128], BF16)
    neg_sl = sb("neg_sl", [128, 128], F32)
    caus_t = sb("caus_t", [128, 128], F32)
    onerow_f = sb("onerow_f", [1, 128], F32)
    onerow_b = sb("onerow_b", [1, 128], BF16)
    onecol_b = sb("onecol_b", [128, 1], BF16)
    h4_sb = sb("h4_sb", [128, 16], BF16)
    ind4r = sb("ind4r", [4, 512], BF16)
    rms_row = sb("rms_row", [1, DV], BF16)

    rslot = [sb("rslot0", [128, 4096], BF16), sb("rslot1", [128, 4096], BF16)]

    xstage = [sb("xstage0", [128, 512], F32), sb("xstage1", [128, 512], F32)]
    x_bf = sb("x_bf", [128, 4096], BF16)
    xn = sb("xn", [128, 4096], BF16)
    xa = sb("xa", [128, 4096], BF16)
    lnrow = sb("lnrow", [1, 2048], BF16)
    cin = sb("cin", [128, 520], BF16)
    acc_cv = sb("acc_cv", [128, 1024], BF16)
    hq_h = sb("hq_h", [128, 24], BF16)
    hk_h = sb("hk_h", [128, 24], BF16)
    hv_h = sb("hv_h", [128, 24], BF16)
    q_buf = sb("q_buf", [128, 4096], BF16)
    k_buf = sb("k_buf", [128, 4096], BF16)
    v_buf = sb("v_buf", [128, 8 * 575], BF16)
    lsll = sb("lsll", [128, 8192], BF16)
    statA = sb("statA", [64, TS], BF16)
    statB = sb("statB", [64, TS], BF16)
    rowt1 = sb("rowt1", [4, TS], F32)
    rowt2 = sb("rowt2", [4, TS], F32)
    statA_tm = sb("statA_tm", [128, 4 * 64], F32)
    statB_tm = sb("statB_tm", [128, 4 * 64], F32)
    q2s = [sb("q2s0", [128, 512], BF16), sb("q2s1", [128, 512], BF16)]
    kn_fm = sb("kn_fm", [128, 256], BF16)
    kn_tm = sb("kn_tm", [128, 256], BF16)
    v_tm = sb("v_tm", [128, 256], BF16)
    ls_tm = sb("ls_tm", [128, 256], BF16)
    ll_tm = sb("ll_tm", [128, 256], BF16)
    L_sb = sb("L_sb", [128, 128], BF16)
    M_sb = sb("M_sb", [128, 128], BF16)
    Xb = [sb("Xb0", [128, 128], BF16), sb("Xb1", [128, 128], BF16)]
    Yb = [sb("Yb0", [128, 128], BF16), sb("Yb1", [128, 128], BF16)]
    Pb = [sb("Pb0", [128, 128], BF16), sb("Pb1", [128, 128], BF16)]
    Tn_sb = sb("Tn_sb", [128, 128], BF16)
    at_sb = sb("at_sb", [128, 128], BF16)
    wt_sb = sb("wt_sb", [128, 256], BF16)
    g_sb = sb("g_sb", [128, 256], BF16)
    S_bf = [sb(f"S_bf{h}", [128, 512], BF16) for h in range(NH)]
    gt = [sb("gt0", [128, 256], BF16), sb("gt1", [128, 256], BF16)]
    sqs = sb("sqs", [128, 256], BF16)
    colf = sb("colf", [128, 16], F32)
    onT = sb("onT", [128, 4096], BF16)
    memb = [sb("memb0", [128, FT], BF16), sb("memb1", [128, FT], BF16)]
    yb_sb = sb("yb_sb", [128, FT], BF16)
    zb = sb("zb", [128, FT], BF16)
    accb = sb("accb", [128, FT], BF16)
    ostage = [sb("ostage0", [128, FT], F32), sb("ostage1", [128, FT], F32)]

    S_ps = [ps(f"S_ps{h}", [128, 512], F32) for h in range(NH)]
    gps = ps("gps", [128, 512], F32)
    work = ps("work", [128, 1024], BF16)
    pj = [ps("pj0", [128, 512], F32), ps("pj1", [128, 512], F32)]

    B = {}

    def nb(name):
        B[name] = Bf()
        return B[name]

    for n in ("wq wk wv wb wf2 bf1T bf2 rmswbc fq fk fv fs fl consts rmsrow "
              "x_bf xn xa lnrow cin acc hq hk hv qb kb vb ls ll statA statB "
              "rowt1 rowt2 statAtm statBtm knfm kntm vtm lstm lltm Lsb Msb "
              "Tn atsb wtsb gsb sqs colf onT zb accb gps_lo gps_hi").split():
        nb(n)
    B["gps_lo"] = B["gps_hi"] = Bf(x=True)
    xstB = [nb("xst0"), nb("xst1")]
    q2B = [nb("q2s0"), nb("q2s1")]
    XB = [nb("Xb0"), nb("Xb1")]
    YB = [nb("Yb0"), nb("Yb1")]
    PB = [nb("Pb0"), nb("Pb1")]
    SbfB = [nb(f"Sbf{h}") for h in range(NH)]
    SpsB = [Bf(x=True) for h in range(NH)]
    gtB = [nb("gt0"), nb("gt1")]
    ybB = nb("yb")
    membB = [nb("memb0"), nb("memb1")]
    ostB = [nb("ost0"), nb("ost1")]
    rslB = [nb("rsl0"), nb("rsl1")]
    _pj0, _pj1, _wk = Bf(x=True), Bf(x=True), Bf(x=True)
    pjB = [[_pj0] * 4, [_pj1] * 4]
    wkB = [_wk] * 8
    scB = nb("scratch_dram")
    outB = nb("outB")

    def mm(out_ap, lhsT, rhs, r, w, start=True, stop=True):
        return op("pe", lambda e, o=out_ap, a=lhsT, b=rhs, s1=start, s2=stop:
                  e.matmul(o, a, b, start=s1, stop=s2, skip_group_check=True), r=r, w=w)

    def tp(out_ap, in_ap, ident, r, w):
        return op("pe", lambda e, o=out_ap, i=in_ap, d=ident: e.transpose(o, i, d), r=r, w=w)

    def acp(out_ap, in_ap, r, w, scale=None, func=AF.Copy, bias=0.0):
        if scale is None:
            return op("act", lambda e, o=out_ap, i=in_ap: e.copy(o, i), r=r, w=w)
        return op("act", lambda e, o=out_ap, i=in_ap, s=scale, f=func, bb=bias:
                  e.activation(o, i, f, bias=bb, scale=s), r=r, w=w)

    def act(out_ap, in_ap, func, r, w, bias=0.0, scale=1.0):
        return op("act", lambda e, o=out_ap, i=in_ap, f=func, bb=bias, s=scale:
                  e.activation(o, i, f, bias=bb, scale=s), r=r, w=w)

    def tt(out_ap, a, b, alu, r, w):
        return op("dve", lambda e, o=out_ap, x=a, y=b, u=alu: e.tensor_tensor(o, x, y, u), r=r, w=w)

    def ts(out_ap, a, s1, o0, r, w, s2=None, o1=ALU.bypass, accum=None):
        return op("dve", lambda e, o=out_ap, x=a, a1=s1, a2=s2, u0=o0, u1=o1, ac=accum:
                  e.tensor_scalar(o, x, a1, a2, u0, u1, accum_out=ac), r=r, w=w)

    def stt(out_ap, in0, s, in1, o0, o1, r, w, accum=None):
        return op("dve", lambda e, o=out_ap, x=in0, a1=s, y=in1, u0=o0, u1=o1, ac=accum:
                  e.scalar_tensor_tensor(o, x, a1, y, u0, u1, accum_out=ac), r=r, w=w)

    def dma(eng, out_ap, in_ap, r, w):
        return op(eng, lambda e, o=out_ap, i=in_ap: e.dma_start(o, i), r=r, w=w, dma=True)

    CB = B["consts"]

    dma("sin", id_f[:, :], ID_F, r=(), w=(CB,))
    dma("sin", id_b[:, :], ID_B, r=(), w=(CB,))
    dma("sin", neg_sl[:, :], NEG_SL, r=(), w=(CB,))
    dma("sin", caus_t[:, :], CAUS_T, r=(), w=(CB,))
    dma("sin", onerow_f[:, :], ONEROW_F, r=(), w=(CB,))
    dma("sin", onerow_b[:, :], ONEROW_B, r=(), w=(CB,))
    dma("sin", onecol_b[:, :], ONECOL_B, r=(), w=(CB,))
    dma("sin", h4_sb[:, :], H4, r=(), w=(CB,))
    dma("sin", ind4r[:, :], IND4R, r=(), w=(CB,))
    dma("sin", bf2_sb[:, :], bf2.rearrange("(a b) -> a b", b=1), r=(), w=(B["bf2"],))
    for kt in range(KT):
        dma("sin", fq_sb[:, kt * 4:(kt + 1) * 4], fq[kt * 128:(kt + 1) * 128, :], r=(), w=(B["fq"],))
        dma("sin", fk_sb[:, kt * 4:(kt + 1) * 4], fk[kt * 128:(kt + 1) * 128, :], r=(), w=(B["fk"],))
        dma("sin", fv_sb[:, kt * 4:(kt + 1) * 4], fv[kt * 128:(kt + 1) * 128, :], r=(), w=(B["fv"],))
        dma("sin", fs_sb[:, kt * 7:(kt + 1) * 7], fs[kt * 128:(kt + 1) * 128, :], r=(), w=(B["fs"],))
        dma("sin", fl_sb[:, kt * 64:(kt + 1) * 64], fl[kt * 128:(kt + 1) * 128, :], r=(), w=(B["fl"],))
    op("dve", lambda e: e.memset(v_buf[:, :], 0.0), w=(B["vb"],))
    op("dve", lambda e: e.memset(hq_h[:, :], 0.0), w=(B["hq"],))
    op("dve", lambda e: e.memset(hk_h[:, :], 0.0), w=(B["hk"],))
    op("dve", lambda e: e.memset(hv_h[:, :], 0.0), w=(B["hv"],))

    pp = [0]

    def conv_chunk(src_ap, dst_sbuf_ap, wbuf, n):
        i = pp[0] % 2
        pp[0] += 1
        dma("sin", xstage[i][:, 0:n], src_ap, r=(), w=(xstB[i],))
        acp(dst_sbuf_ap, xstage[i][:, 0:n], r=(xstB[i], CB), w=(wbuf,))

    def conv_chunk_dram(src_ap, dst_dram_ap, n):
        i = pp[0] % 2
        pp[0] += 1
        dma("sin", xstage[i][:, 0:n], src_ap, r=(), w=(xstB[i],))
        j = pp[0] % 2
        pp[0] += 1
        acp(q2s[j][:, 0:n], xstage[i][:, 0:n], r=(xstB[i],), w=(q2B[j],))
        dma("sin", dst_dram_ap, q2s[j][:, 0:n], r=(q2B[j],), w=(scB,))

    for (Wsrc, wdst, wbuf) in ((Wq, wq_sb, B["wq"]), (Wk, wk_sb, B["wk"]), (Wv, wv_sb, B["wv"])):
        for kt in range(KT):
            for hh in range(2):
                conv_chunk(Wsrc[kt * 128:(kt + 1) * 128, hh * 512:(hh + 1) * 512],
                           wdst[:, kt * 1024 + hh * 512: kt * 1024 + (hh + 1) * 512], wbuf, 512)
    for kt in range(KT):
        dma("sin", xstage[0][:, kt * 4:(kt + 1) * 4], Wb[kt * 128:(kt + 1) * 128, :], r=(B["wb"],), w=(xstB[0],))
    acp(wb_sb[:, :], xstage[0][:, 0:32], r=(xstB[0],), w=(B["wb"],))
    for kt in range(16):
        dma("sin", xstage[1][:, kt * 16:(kt + 1) * 16], Wf2[kt * 128:(kt + 1) * 128, :], r=(B["wf2"],), w=(xstB[1],))
    acp(wf2_sb[:, :], xstage[1][:, 0:256], r=(xstB[1],), w=(B["wf2"],))
    dma("sin", xstage[0][0:16, 0:128], bf1.rearrange("(a b) -> a b", b=128), r=(xstB[0],), w=(xstB[0],))
    tp(pj[0][0:128, 0:16], xstage[0][0:16, 0:128], id_f[0:16, 0:16], r=(xstB[0], CB), w=(pjB[0][0],))
    acp(bf1T[:, :], pj[0][0:128, 0:16], r=(pjB[0][0],), w=(B["bf1T"],))
    dma("sin", xstage[0][0:1, 0:DV], rmsw.rearrange("(a b) -> a b", a=1), r=(xstB[0],), w=(xstB[0],))
    acp(rms_row[0:1, :], xstage[0][0:1, 0:DV], r=(xstB[0],), w=(B["rmsrow"],))
    mm(pj[1][:, 0:DV], onerow_b[0:1, :], rms_row[0:1, :], r=(CB, B["rmsrow"]), w=(pjB[1][0], pjB[1][1]))
    acp(rmsw_bc[:, :], pj[1][:, 0:DV], r=(pjB[1][0], pjB[1][1]), w=(B["rmswbc"],))
    for p in range(4):
        for kt in range(KT):
            conv_chunk_dram(Wf1[kt * 128:(kt + 1) * 128, p * 512:(p + 1) * 512],
                            wf1_sc[:, p * 4096 + kt * 512: p * 4096 + (kt + 1) * 512], 512)
    for p in range(2):
        for kt in range(KT):
            conv_chunk_dram(Wo[kt * 128:(kt + 1) * 128, p * 512:(p + 1) * 512],
                            wo_sc[:, p * 4096 + kt * 512: p * 4096 + (kt + 1) * 512], 512)
    for p in range(8):
        for kt in range(KT):
            conv_chunk_dram(WfcT[kt * 128:(kt + 1) * 128, p * 512:(p + 1) * 512],
                            wfc_sc[:, p * 4096 + kt * 512: p * 4096 + (kt + 1) * 512], 512)
    for p in range(8):
        for kt in range(32):
            conv_chunk_dram(WprojT[kt * 128:(kt + 1) * 128, p * 128:(p + 1) * 128],
                            wpj_sc[:, p * 4096 + kt * 128: p * 4096 + (kt + 1) * 128], 128)

    gcount = [0]

    LASTGC = nseg * NCH - 1
    for seg in range(nseg):
        s0 = seg * TS

        for mt in range(KT):
            i = mt % 2
            dma("sin", xstage[i][:, :], xT[mt * 128:(mt + 1) * 128, s0:s0 + TS], r=(), w=(xstB[i],))
            acp(x_bf[:, mt * 512:(mt + 1) * 512], xstage[i][:, :], r=(xstB[i],), w=(B["x_bf"],))

        pieces = [("wf1", 0, 0), ("wf1", 1, 0), ("wf1", 2, 0), ("wf1", 3, 0),
                  ("wo", 0, 0), ("wo", 1, 0)]
        for _th in range(TS // FT):
            pieces += [("wfc", p, _th) for p in range(8)] + [("wpj", p, _th) for p in range(8)]
        scmap = {"wf1": wf1_sc, "wo": wo_sc, "wfc": wfc_sc, "wpj": wpj_sc}
        pidx = {tp: i for i, tp in enumerate(pieces)}
        ploaded = [0]

        def pload(j):
            while ploaded[0] <= j:
                jj = ploaded[0]
                tag, p, _t = pieces[jj]
                i = jj % 2
                dma("sin", rslot[i][:, :], scmap[tag][:, p * 4096:(p + 1) * 4096],
                    r=(scB,), w=(rslB[i],))
                ploaded[0] = jj + 1

        def pget(tag, p, th=0):
            j = pidx[(tag, p, th)]
            pload(min(j + 1, len(pieces) - 1))
            return j % 2

        pload(0)

        for kt in range(KT):
            i = kt % 2
            act(q2s[i][:, :], x_bf[:, kt * 512:(kt + 1) * 512], AF.Square, r=(B["x_bf"],), w=(q2B[i],))
            mm(pj[0][0:1, 0:TS], onecol_b[:, 0:1], x_bf[:, kt * 512:(kt + 1) * 512],
               r=(CB, B["x_bf"]), w=tuple(pjB[0]), start=(kt == 0), stop=(kt == KT - 1))
            mm(pj[1][0:1, 0:TS], onecol_b[:, 0:1], q2s[i][:, :],
               r=(CB, q2B[i]), w=tuple(pjB[1]), start=(kt == 0), stop=(kt == KT - 1))
        ts(lnrow[0:1, 0:512], pj[0][0:1, 0:TS], 1.0 / H, ALU.mult, r=tuple(pjB[0]), w=(B["lnrow"],))
        ts(lnrow[0:1, 512:1024], pj[1][0:1, 0:TS], 1.0 / H, ALU.mult, r=tuple(pjB[1]), w=(B["lnrow"],))
        tt(lnrow[0:1, 1024:1536], lnrow[0:1, 0:512], lnrow[0:1, 0:512], ALU.mult, r=(B["lnrow"],), w=(B["lnrow"],))
        stt(lnrow[0:1, 1536:2048], lnrow[0:1, 512:1024], 1e-5, lnrow[0:1, 1024:1536], ALU.add, ALU.subtract,
            r=(B["lnrow"],), w=(B["lnrow"],))
        act(lnrow[0:1, 1024:1536], lnrow[0:1, 1536:2048], AF.Ln, r=(B["lnrow"],), w=(B["lnrow"],))
        ts(lnrow[0:1, 1536:2048], lnrow[0:1, 1024:1536], -0.5, ALU.mult, r=(B["lnrow"],), w=(B["lnrow"],))
        act(lnrow[0:1, 1024:1536], lnrow[0:1, 1536:2048], AF.Exp, r=(B["lnrow"],), w=(B["lnrow"],))
        mm(pj[0][:, 0:TS], onerow_b[0:1, :], lnrow[0:1, 0:512], r=(CB, B["lnrow"]), w=tuple(pjB[0]))
        mm(pj[1][:, 0:TS], onerow_b[0:1, :], lnrow[0:1, 1024:1536], r=(CB, B["lnrow"]), w=tuple(pjB[1]))
        acp(q2s[0][:, :], pj[0][:, 0:TS], r=tuple(pjB[0]), w=(q2B[0],))
        acp(q2s[1][:, :], pj[1][:, 0:TS], r=tuple(pjB[1]), w=(q2B[1],))
        for kt in range(KT):
            tt(xn[:, kt * 512:(kt + 1) * 512], x_bf[:, kt * 512:(kt + 1) * 512], q2s[0][:, :], ALU.subtract,
               r=(B["x_bf"], q2B[0]), w=(B["xn"],))
            tt(xn[:, kt * 512:(kt + 1) * 512], xn[:, kt * 512:(kt + 1) * 512], q2s[1][:, :], ALU.mult,
               r=(B["xn"], q2B[1]), w=(B["xn"],))

        for kt in range(KT):
            mm(pj[0][0:4, 0:TS], wb_sb[:, kt * 4:(kt + 1) * 4], xn[:, kt * 512:(kt + 1) * 512],
               r=(B["wb"], B["xn"]), w=tuple(pjB[0]), start=(kt == 0), stop=(kt == KT - 1))
        act(statA[0:4, :], pj[0][0:4, 0:TS], AF.Sigmoid, r=tuple(pjB[0]), w=(B["statA"],))

        for (wt, wbuf, fsb, fB, hb, hB, obuf, oB) in (
                (wq_sb, B["wq"], fq_sb, B["fq"], hq_h, B["hq"], q_buf, B["qb"]),
                (wk_sb, B["wk"], fk_sb, B["fk"], hk_h, B["hk"], k_buf, B["kb"]),
                (wv_sb, B["wv"], fv_sb, B["fv"], hv_h, B["hv"], v_buf, B["vb"])):
            for mt in range(KT):
                bank = mt % 2
                for kt in range(KT):
                    mm(pj[bank][:, 0:TS], wt[:, kt * 1024 + mt * 128: kt * 1024 + (mt + 1) * 128],
                       xn[:, kt * 512:(kt + 1) * 512], r=(wbuf, B["xn"]), w=tuple(pjB[bank]),
                       start=(kt == 0), stop=(kt == KT - 1))
                acp(cin[:, 0:3], hb[:, mt * 3:(mt + 1) * 3], r=(hB,), w=(B["cin"],))
                acp(cin[:, 3:515], pj[bank][:, 0:TS], r=tuple(pjB[bank]), w=(B["cin"],))
                acp(hb[:, mt * 3:(mt + 1) * 3], cin[:, 512:515], r=(B["cin"],), w=(hB,))
                ts(acc_cv[:, 0:512], cin[:, 0:512], fsb[:, mt * 4:mt * 4 + 1], ALU.mult,
                   r=(B["cin"], fB), w=(B["acc"],))
                for j in range(1, 4):
                    src = acc_cv[:, 0:512] if j % 2 == 1 else acc_cv[:, 512:1024]
                    dst = acc_cv[:, 512:1024] if j % 2 == 1 else acc_cv[:, 0:512]
                    stt(dst, cin[:, j:j + 512], fsb[:, mt * 4 + j:mt * 4 + j + 1], src, ALU.mult, ALU.add,
                        r=(B["cin"], fB, B["acc"]), w=(B["acc"],))
                if obuf is v_buf:
                    acp(v_buf[:, mt * 575:mt * 575 + 63], v_buf[:, mt * 575 + 512:mt * 575 + 575],
                        r=(oB,), w=(oB,))
                    act(v_buf[:, mt * 575 + 63:(mt + 1) * 575], acc_cv[:, 512:1024], AF.Silu,
                        r=(B["acc"],), w=(oB,))
                else:
                    act(obuf[:, mt * 512:(mt + 1) * 512], acc_cv[:, 512:1024], AF.Silu,
                        r=(B["acc"],), w=(oB,))

        for (srcb, srcB_, off, rows, negd) in ((q_buf, B["qb"], 0, statB[32:36, :], True),
                                               (k_buf, B["kb"], 0, statB[0:4, :], False)):
            for kt in range(KT):
                i = kt % 2
                act(q2s[i][:, :], srcb[:, kt * 512:(kt + 1) * 512], AF.Square, r=(srcB_,), w=(q2B[i],))
                mm(pj[0][0:4, 0:TS], h4_sb[:, 4 * (kt // 2):4 * (kt // 2) + 4], q2s[i][:, :],
                   r=(CB, q2B[i]), w=tuple(pjB[0]), start=(kt == 0), stop=(kt == KT - 1))
            act(rowt1[0:4, :], pj[0][0:4, 0:TS], AF.Ln, r=tuple(pjB[0]), w=(B["rowt1"],))
            ts(rowt2[0:4, :], rowt1[0:4, :], -0.5, ALU.mult, r=(B["rowt1"],), w=(B["rowt2"],))
            if negd:
                act(rowt1[0:4, :], rowt2[0:4, :], AF.Exp, r=(B["rowt2"],), w=(B["rowt1"],))
                ts(rows, rowt1[0:4, :], -1.0, ALU.mult, r=(B["rowt1"],), w=(B["statB"],))
            else:
                act(rows, rowt2[0:4, :], AF.Exp, r=(B["rowt2"],), w=(B["statB"],))

        for mt in range(16):
            bank = mt % 2
            p = mt // 4
            sl = pget("wf1", p)
            for kt in range(KT):
                mm(pj[bank][:, 0:TS], rslot[sl][:, kt * 512 + (mt % 4) * 128: kt * 512 + (mt % 4 + 1) * 128],
                   xn[:, kt * 512:(kt + 1) * 512], r=(rslB[sl], B["xn"]), w=tuple(pjB[bank]),
                   start=(kt == 0), stop=(kt == KT - 1))
            i = mt % 2
            act(q2s[i][:, :], pj[bank][:, 0:TS], AF.Gelu, r=tuple(pjB[bank]), w=(q2B[i],),
                bias=bf1T[:, mt:mt + 1])
            mm(gps[0:16, 0:TS], wf2_sb[:, mt * 16:(mt + 1) * 16], q2s[i][:, :],
               r=(B["wf2"], q2B[i], B["bf1T"]), w=(B["gps_lo"], B["gps_hi"]),
               start=(mt == 0), stop=(mt == 15))
        act(statA[32:48, :], gps[0:16, 0:TS], AF.Exp, r=(B["gps_lo"], B["gps_hi"]), w=(B["statA"],),
            bias=bf2_sb[0:16, 0:1])

        for h in range(NH):
            for mt in (2 * h, 2 * h + 1):
                base = mt * 575
                for (nt, fsb2, fB2, outsl) in ((7, fs_sb, B["fs"], lsll[:, mt * 512:(mt + 1) * 512]),
                                               (64, fl_sb, B["fl"], lsll[:, 4096 + mt * 512:4096 + (mt + 1) * 512])):
                    sh0 = 57 if nt == 7 else 0
                    ts(acc_cv[:, 0:512], v_buf[:, base + sh0:base + sh0 + 512],
                       fsb2[:, mt * nt:mt * nt + 1], ALU.mult, r=(B["vb"], fB2), w=(B["acc"],))
                    for j in range(1, nt):
                        src = acc_cv[:, 0:512] if j % 2 == 1 else acc_cv[:, 512:1024]
                        dst = acc_cv[:, 512:1024] if j % 2 == 1 else acc_cv[:, 0:512]
                        if j == nt - 1:
                            dst = outsl
                        wb_ = (B["ls"] if nt == 7 else B["ll"],) if j == nt - 1 else (B["acc"],)
                        stt(dst, v_buf[:, base + sh0 + j:base + sh0 + j + 512],
                            fsb2[:, mt * nt + j:mt * nt + j + 1], src, ALU.mult, ALU.add,
                            r=(B["vb"], fB2, B["acc"]), w=wb_)

            for c in range(NCH):
                gc = gcount[0] + c
                t0 = c * 128
                if h == 0:
                    tp(work[:, 0:64], statA[0:64, t0:t0 + 128], id_b[0:64, 0:64],
                       r=(B["statA"], CB), w=(wkB[0],))
                    acp(statA_tm[:, c * 64:(c + 1) * 64], work[:, 0:64], r=(wkB[0],), w=(B["statAtm"],))
                    tp(work[:, 128:192], statB[0:64, t0:t0 + 128], id_b[0:64, 0:64],
                       r=(B["statB"], CB), w=(wkB[1],))
                    acp(statB_tm[:, c * 64:(c + 1) * 64], work[:, 128:192], r=(wkB[1],), w=(B["statBtm"],))
                beta_c = statA_tm[:, c * 64 + h:c * 64 + h + 1]
                rk_bc_r = (B["statBtm"],)
                mm(pj[1][:, 384:512], ind4r[:, h * 128:(h + 1) * 128], statB[0:4, t0:t0 + 128],
                   r=(CB, B["statB"]), w=(pjB[1][3],))
                for d in range(2):
                    tt(kn_fm[:, d * 128:(d + 1) * 128], k_buf[:, (2 * h + d) * 512 + t0:(2 * h + d) * 512 + t0 + 128],
                       pj[1][:, 384:512], ALU.mult, r=(B["kb"], pjB[1][3]), w=(B["knfm"],))
                mm(pj[0][:, 384:512], kn_fm[:, 0:128], kn_fm[:, 0:128], r=(B["knfm"],), w=(pjB[0][3],),
                   start=True, stop=False)
                mm(pj[0][:, 384:512], kn_fm[:, 128:256], kn_fm[:, 128:256], r=(B["knfm"],), w=(pjB[0][3],),
                   start=False, stop=True)
                stt(L_sb[:, :], pj[0][:, 384:512], beta_c, neg_sl[:, :], ALU.mult, ALU.mult,
                    r=(pjB[0][3], B["statAtm"], CB), w=(B["Lsb"],))
                tp(work[:, 256:384], L_sb[:, :], id_b[:, :], r=(B["Lsb"], CB), w=(wkB[2],))
                acp(M_sb[:, :], work[:, 256:384], r=(wkB[2],), w=(B["Msb"],))
                tt(Pb[0][:, :], M_sb[:, :], id_b[:, :], ALU.add, r=(B["Msb"], CB), w=(PB[0],))
                Xc, Yc, XcB, YcB = L_sb, M_sb, B["Lsb"], B["Msb"]
                for j in range(6):
                    nx, ny = Xb[j % 2], Yb[j % 2]
                    nxB, nyB = XB[j % 2], YB[j % 2]
                    mm(pj[0][:, 0:128], Yc[:, :], Xc[:, :], r=(YcB, XcB), w=(pjB[0][0],))
                    acp(nx[:, :], pj[0][:, 0:128], r=(pjB[0][0],), w=(nxB,))
                    if j < 5:
                        mm(pj[0][:, 128:256], Xc[:, :], Yc[:, :], r=(XcB, YcB), w=(pjB[0][1],))
                        acp(ny[:, :], pj[0][:, 128:256], r=(pjB[0][1],), w=(nyB,))
                    mm(pj[0][:, 256:384], nx[:, :], Pb[j % 2][:, :], r=(nxB, PB[j % 2]), w=(pjB[0][2],))
                    tt(Pb[(j + 1) % 2][:, :], pj[0][:, 256:384], Pb[j % 2][:, :], ALU.add,
                       r=(pjB[0][2], PB[j % 2]), w=(PB[(j + 1) % 2],))
                    Xc, Yc, XcB, YcB = nx, ny, nxB, nyB
                TT_, TTB = Pb[0], PB[0]
                ts(Tn_sb[:, :], TT_[:, :], beta_c, ALU.mult, r=(TTB, B["statAtm"]), w=(B["Tn"],),
                   s2=-1.0, o1=ALU.mult)
                for d in range(2):
                    mm(pj[1][:, 0:128], kn_fm[:, d * 128:(d + 1) * 128],
                       q_buf[:, (2 * h + d) * 512 + t0:(2 * h + d) * 512 + t0 + 128],
                       r=(B["knfm"], B["qb"]), w=(pjB[1][0],), start=(d == 0), stop=(d == 1))
                tt(at_sb[:, :], pj[1][:, 0:128], caus_t[:, :], ALU.mult, r=(pjB[1][0], CB), w=(B["atsb"],))
                for d in range(2):
                    tp(work[:, (4 + d) * 128:(5 + d) * 128], kn_fm[:, d * 128:(d + 1) * 128], id_b[:, :],
                       r=(B["knfm"], CB), w=(wkB[4 + d],))
                    acp(kn_tm[:, d * 128:(d + 1) * 128], work[:, (4 + d) * 128:(5 + d) * 128],
                        r=(wkB[4 + d],), w=(B["kntm"],))
                    tp(work[:, (6 + d) * 128:(7 + d) * 128],
                       v_buf[:, (2 * h + d) * 575 + 63 + t0:(2 * h + d) * 575 + 63 + t0 + 128], id_b[:, :],
                       r=(B["vb"], CB), w=(wkB[6 + d],))
                    acp(v_tm[:, d * 128:(d + 1) * 128], work[:, (6 + d) * 128:(7 + d) * 128],
                        r=(wkB[6 + d],), w=(B["vtm"],))
                for d in range(2):
                    mm(pj[1][:, 128 + d * 128:256 + d * 128], kn_tm[:, d * 128:(d + 1) * 128], Tn_sb[:, :],
                       r=(B["kntm"], B["Tn"]), w=(pjB[1][1 + d],))
                    acp(wt_sb[:, d * 128:(d + 1) * 128], pj[1][:, 128 + d * 128:256 + d * 128],
                        r=(pjB[1][1 + d],), w=(B["wtsb"],))
                if gc > 0:
                    for d in range(2):
                        mm(gps[:, 0:256], wt_sb[:, d * 128:(d + 1) * 128], S_bf[h][:, d * 256:(d + 1) * 256],
                           r=(B["wtsb"], SbfB[h]), w=(B["gps_lo"],), start=(d == 0), stop=False)
                    mm(gps[:, 0:256], Tn_sb[:, :], v_tm[:, :], r=(B["Tn"], B["vtm"]), w=(B["gps_lo"],),
                       start=False, stop=True)
                else:
                    mm(gps[:, 0:256], Tn_sb[:, :], v_tm[:, :], r=(B["Tn"], B["vtm"]), w=(B["gps_lo"],))
                acp(g_sb[:, :], gps[:, 0:256], r=(B["gps_lo"],), w=(B["gsb"],))
                if gc > 0:
                    for d in range(2):
                        mm(gps[:, 256:512], q_buf[:, (2 * h + d) * 512 + t0:(2 * h + d) * 512 + t0 + 128],
                           S_bf[h][:, d * 256:(d + 1) * 256], r=(B["qb"], SbfB[h]), w=(B["gps_hi"],),
                           start=(d == 0), stop=False)
                    mm(gps[:, 256:512], at_sb[:, :], g_sb[:, :], r=(B["atsb"], B["gsb"]), w=(B["gps_hi"],),
                       start=False, stop=True)
                else:
                    mm(gps[:, 256:512], at_sb[:, :], g_sb[:, :], r=(B["atsb"], B["gsb"]), w=(B["gps_hi"],))
                for d in range(2):
                    mm(S_ps[h][:, d * 256:(d + 1) * 256], kn_tm[:, d * 128:(d + 1) * 128], g_sb[:, :],
                       r=(B["kntm"], B["gsb"]), w=(SpsB[h],), start=(gc == 0 and d == 0), stop=True)
                if gc < LASTGC:
                    acp(S_bf[h][:, :], S_ps[h][:, :], r=(SpsB[h],), w=(SbfB[h],))
                for d in range(2):
                    tp(work[:, d * 128:(d + 1) * 128],
                       lsll[:, (2 * h + d) * 512 + t0:(2 * h + d) * 512 + t0 + 128], id_b[:, :],
                       r=(B["ls"], CB), w=(wkB[d],))
                    acp(ls_tm[:, d * 128:(d + 1) * 128], work[:, d * 128:(d + 1) * 128],
                        r=(wkB[d],), w=(B["lstm"],))
                    tp(work[:, (2 + d) * 128:(3 + d) * 128],
                       lsll[:, 4096 + (2 * h + d) * 512 + t0:4096 + (2 * h + d) * 512 + t0 + 128], id_b[:, :],
                       r=(B["ll"], CB), w=(wkB[2 + d],))
                    acp(ll_tm[:, d * 128:(d + 1) * 128], work[:, (2 + d) * 128:(3 + d) * 128],
                        r=(wkB[2 + d],), w=(B["lltm"],))
                e_ = [statA_tm[:, c * 64 + 32 + 4 * h + i:c * 64 + 32 + 4 * h + i + 1] for i in range(4)]
                nrq = statB_tm[:, c * 64 + 32 + h:c * 64 + 32 + h + 1]
                tt(colf[:, 0:1], e_[0], e_[1], ALU.add, r=(B["statAtm"],), w=(B["colf"],))
                tt(colf[:, 1:2], e_[2], e_[3], ALU.add, r=(B["statAtm"],), w=(B["colf"],))
                tt(colf[:, 2:3], colf[:, 0:1], colf[:, 1:2], ALU.add, r=(B["colf"],), w=(B["colf"],))
                op("dve", lambda e, o=colf[:, 3:4], i=colf[:, 2:3]: e.reciprocal(o, i),
                   r=(B["colf"],), w=(B["colf"],))
                r_c = colf[:, 3:4]
                ts(colf[:, 4:5], e_[0], r_c, ALU.mult, r=(B["statAtm"], B["colf"]), w=(B["colf"],))
                ts(colf[:, 5:6], e_[1], r_c, ALU.mult, r=(B["statAtm"], B["colf"]), w=(B["colf"],))
                ts(colf[:, 6:7], e_[2], r_c, ALU.mult, r=(B["statAtm"], B["colf"]), w=(B["colf"],),
                   s2=nrq, o1=ALU.mult)
                ts(colf[:, 7:8], e_[3], r_c, ALU.mult, r=(B["statAtm"], B["colf"]), w=(B["colf"],))
                ts(gt[0][:, :], ls_tm[:, :], colf[:, 4:5], ALU.mult, r=(B["lstm"], B["colf"]), w=(gtB[0],))
                stt(gt[1][:, :], ll_tm[:, :], colf[:, 5:6], gt[0][:, :], ALU.mult, ALU.add,
                    r=(B["lltm"], B["colf"], gtB[0]), w=(gtB[1],))
                stt(gt[0][:, :], gps[:, 256:512], colf[:, 6:7], gt[1][:, :], ALU.mult, ALU.add,
                    r=(B["gps_hi"], B["colf"], gtB[1]), w=(gtB[0],))
                stt(gt[1][:, :], v_tm[:, :], colf[:, 7:8], gt[0][:, :], ALU.mult, ALU.add,
                    r=(B["vtm"], B["colf"], gtB[0]), w=(gtB[1],))
                stt(sqs[:, :], gt[1][:, :], 1.0, gt[1][:, :], ALU.mult, ALU.mult,
                    r=(gtB[1],), w=(B["sqs"], B["colf"]), accum=colf[:, 8:9])
                ts(colf[:, 9:10], colf[:, 8:9], 1.0 / DV, ALU.mult, r=(B["colf"],), w=(B["colf"],),
                   s2=1e-5, o1=ALU.add)
                act(colf[:, 10:11], colf[:, 9:10], AF.Ln, r=(B["colf"],), w=(B["colf"],))
                ts(colf[:, 11:12], colf[:, 10:11], -0.5, ALU.mult, r=(B["colf"],), w=(B["colf"],))
                act(colf[:, 12:13], colf[:, 11:12], AF.Exp, r=(B["colf"],), w=(B["colf"],))
                stt(gt[0][:, :], gt[1][:, :], colf[:, 12:13], rmsw_bc[:, :], ALU.mult, ALU.mult,
                    r=(gtB[1], B["colf"], B["rmswbc"]), w=(gtB[0],))
                for d in range(2):
                    tp(work[:, (4 + d) * 128:(5 + d) * 128], gt[0][:, d * 128:(d + 1) * 128], id_b[:, :],
                       r=(gtB[0], CB), w=(wkB[4 + d],))
                    acp(onT[:, (2 * h + d) * 512 + t0:(2 * h + d) * 512 + t0 + 128],
                        work[:, (4 + d) * 128:(5 + d) * 128], r=(wkB[4 + d],), w=(B["onT"],))
        gcount[0] += NCH

        for mt in range(KT):
            bank = mt % 2
            p = mt // 4
            sl = pget("wo", p)
            for kt in range(KT):
                mm(pj[bank][:, 0:TS], rslot[sl][:, kt * 512 + (mt % 4) * 128: kt * 512 + (mt % 4 + 1) * 128],
                   onT[:, kt * 512:(kt + 1) * 512], r=(rslB[sl], B["onT"]), w=tuple(pjB[bank]),
                   start=(kt == 0), stop=(kt == KT - 1))
            tt(xa[:, mt * 512:(mt + 1) * 512], pj[bank][:, 0:TS], x_bf[:, mt * 512:(mt + 1) * 512],
               ALU.add, r=tuple(pjB[bank]) + (B["x_bf"],), w=(B["xa"],))

        for th in range(TS // FT):
            tof = th * FT
            for fm in range(32):
                bank = fm % 2
                p = fm // 4
                sl = pget("wfc", p, th)
                for kt in range(KT):
                    mm(pj[bank][:, 0:FT], rslot[sl][:, kt * 512 + (fm % 4) * 128: kt * 512 + (fm % 4 + 1) * 128],
                       xn[:, kt * 512 + tof: kt * 512 + tof + FT], r=(rslB[sl], B["xn"]),
                       w=tuple(pjB[bank]), start=(kt == 0), stop=(kt == KT - 1))
                i = fm % 2
                act(q2s[i][:, 0:FT], pj[bank][:, 0:FT], AF.Relu, r=tuple(pjB[bank]), w=(q2B[i],))
                tt(lsll[:, fm * 256:(fm + 1) * 256], q2s[i][:, 0:FT], q2s[i][:, 0:FT], ALU.mult,
                   r=(q2B[i],), w=(B["ls"], B["ll"]))
            for mt in range(KT):
                bank = mt % 2
                sl = pget("wpj", mt, th)
                for kt in range(32):
                    mm(pj[bank][:, 0:FT], rslot[sl][:, kt * 128:(kt + 1) * 128],
                       lsll[:, kt * 256:(kt + 1) * 256], r=(rslB[sl], B["ls"], B["ll"]),
                       w=tuple(pjB[bank]), start=(kt == 0), stop=(kt == 31))
                acp(yb_sb[:, :], pj[bank][:, 0:FT], r=tuple(pjB[bank]), w=(ybB,))
                ts(accb[:, :], yb_sb[:, :], 1.0, ALU.is_gt, r=(ybB,), w=(B["accb"],))
                prev_ap, prev_B = yb_sb, ybB
                for st in range(4):
                    stt(zb[:, :], prev_ap[:, :], 1.0, yb_sb[:, :], ALU.is_gt, ALU.subtract,
                        r=(prev_B, ybB), w=(B["zb"],))
                    stt(memb[st % 2][:, :], prev_ap[:, :], 0.9, zb[:, :], ALU.mult, ALU.subtract,
                        r=(prev_B, B["zb"]), w=(membB[st % 2],))
                    prev_ap, prev_B = memb[st % 2], membB[st % 2]
                    stt(accb[:, :], prev_ap[:, :], 1.0, accb[:, :], ALU.is_gt, ALU.add,
                        r=(prev_B, B["accb"]), w=(B["accb"],))
                i = mt % 2
                stt(ostage[i][:, :], accb[:, :], 0.2, xa[:, mt * 512 + tof: mt * 512 + tof + FT],
                    ALU.mult, ALU.add, r=(B["accb"], B["xa"]), w=(ostB[i],))
                dma("pool", outT[mt * 128:(mt + 1) * 128, s0 + tof:s0 + tof + FT], ostage[i][:, :],
                    r=(ostB[i],), w=(outB,))

    RDMA = 6
    ops = P.ops
    n = len(ops)
    SELF_SYNC = {"dve", "act", "pool"}
    need = [False] * n
    for k in range(n):
        e, _, deps, inc = ops[k]
        if inc == 16:
            need[k] = True
        for j in deps:
            if ops[j][0] != e or ops[j][0] in SELF_SYNC:
                need[j] = True
    cum = {}
    dcnt = {}
    sem_of = [None] * n
    val_after = [0] * n
    for k in range(n):
        e, _, _, inc = ops[k]
        if inc == 16:
            c = dcnt.get(e, 0)
            sname = f"{e}_d{c % RDMA}"
            dcnt[e] = c + 1
            v = cum.get(sname, 0) + 16
            cum[sname] = v
            sem_of[k] = sname
            val_after[k] = v
        else:
            sname = e
            c0 = cum.get(sname, 0)
            if need[k]:
                c0 += inc
            cum[sname] = c0
            sem_of[k] = sname
            val_after[k] = c0

    sems = {}
    for e in ("pe", "act", "dve", "pool"):
        t = nc.semaphore(f"sem_{e}")
        ctx.append(t)
        sems[e] = t.__enter__()
    for q in ("sin", "pool"):
        for r in range(RDMA):
            t = nc.semaphore(f"sem_{q}_d{r}")
            ctx.append(t)
            sems[f"{q}_d{r}"] = t.__enter__()

    blk = nc.Block()
    ctx.append(blk)
    block = blk.__enter__()

    def make_section(ename):
        def body(eng):
            waited = {}

            def wge(sname, v):
                if v > waited.get(sname, 0):
                    eng.wait_ge(sems[sname], v)
                    waited[sname] = v

            for k in range(n):
                e, fn, deps, inc = ops[k]
                if e != ename:
                    continue
                for j in sorted(deps):
                    if ops[j][0] == ename and ops[j][3] != 16 and ename not in SELF_SYNC:
                        continue
                    if ops[j][0] == ename and ops[j][3] == 16 and inc == 16:
                        continue
                    wge(sem_of[j], val_after[j])
                if inc == 16:
                    wge(sem_of[k], val_after[k] - 16)
                inst = fn(eng)
                if need[k] and inc != 16:
                    inst.then_inc(sems[ename], inc)
                elif inc == 16:
                    inst.then_inc(sems[sem_of[k]], 16)
            if ename == "pool":
                for r in range(RDMA):
                    sname = f"pool_d{r}"
                    if cum.get(sname, 0) > 0:
                        eng.wait_ge(sems[sname], cum[sname])
        return body

    block.tensor(make_section("pe"))
    block.scalar(make_section("act"))
    block.vector(make_section("dve"))
    block.gpsimd(make_section("pool"))
    block.sync(make_section("sin"))

    for t in reversed(ctx):
        t.__exit__(None, None, None)
    return nc


_NC = None


def kernel(**inputs):
    global _NC
    x = np.asarray(inputs["L0_attn_hidden_states"], dtype=np.float32)
    common = {
        "Wq": np.ascontiguousarray(inputs["L0_attn_Wq"], dtype=np.float32),
        "Wk": np.ascontiguousarray(inputs["L0_attn_Wk"], dtype=np.float32),
        "Wv": np.ascontiguousarray(inputs["L0_attn_Wv"], dtype=np.float32),
        "Wb": np.ascontiguousarray(inputs["L0_attn_Wb"], dtype=np.float32),
        "Wf1": np.ascontiguousarray(inputs["L0_attn_Wf1"], dtype=np.float32),
        "bf1": np.ascontiguousarray(inputs["L0_attn_bf1"], dtype=np.float32),
        "Wf2": np.ascontiguousarray(inputs["L0_attn_Wf2"], dtype=np.float32),
        "bf2": np.ascontiguousarray(inputs["L0_attn_bf2"], dtype=np.float32),
        "Wo": np.ascontiguousarray(inputs["L0_attn_Wo"], dtype=np.float32),
        "rmsw": np.ascontiguousarray(inputs["L0_attn_rms_w"], dtype=np.float32),
        "fq": np.ascontiguousarray(inputs["L0_attn_conv_q_w"], dtype=np.float32),
        "fk": np.ascontiguousarray(inputs["L0_attn_conv_k_w"], dtype=np.float32),
        "fv": np.ascontiguousarray(inputs["L0_attn_conv_v_w"], dtype=np.float32),
        "fs": np.ascontiguousarray(np.asarray(inputs["L0_attn_fir_short_w"]).reshape(H, 7), dtype=np.float32),
        "fl": np.ascontiguousarray(np.asarray(inputs["L0_attn_fir_long_w"]).reshape(H, 64), dtype=np.float32),
        "WfcT": np.ascontiguousarray(np.asarray(inputs["L0_ffn_W_fc"], dtype=np.float32).T),
        "WprojT": np.ascontiguousarray(np.asarray(inputs["L0_ffn_W_proj"], dtype=np.float32).T),
        "ID_F": np.eye(128, dtype=np.float32),
        "ID_B": np.eye(128).astype(BF),
        "NEG_SL": np.ascontiguousarray(-np.tril(np.ones((128, 128), dtype=np.float32), -1)),
        "CAUS_T": np.ascontiguousarray(np.triu(np.ones((128, 128), dtype=np.float32))),
        "ONEROW_F": np.ones((1, 128), dtype=np.float32),
        "ONEROW_B": np.ones((1, 128)).astype(BF),
        "ONECOL_B": np.ones((128, 1)).astype(BF),
        "H4": np.tile(np.eye(4, dtype=np.float32).reshape(1, 16), (128, 1)).astype(BF),
        "IND4R": np.kron(np.eye(4, dtype=np.float32), np.ones((1, 128), dtype=np.float32)).astype(BF),
    }
    if _NC is None:
        _NC = build_nc()
    in_maps = []
    for b in range(8):
        m = dict(common)
        m["xT"] = np.ascontiguousarray(x[b].T)
        in_maps.append(m)
    res = run_bass_kernel_spmd(_NC, in_maps, core_ids=list(range(8)))
    out = np.stack([np.ascontiguousarray(np.asarray(r["outT"]).T) for r in res.results], axis=0)
    return out.astype(np.float32)


# version 04 (final)
# speedup vs baseline: 1.0580x; 1.0580x over previous
import numpy as np
import ml_dtypes
import concourse.bass as bass
import concourse.mybir as mybir
from concourse.bass_utils import run_bass_kernel_spmd

F32 = mybir.dt.float32
BF16 = mybir.dt.bfloat16
AF = mybir.ActivationFunctionType
ALU = mybir.AluOpType

L, H, NH = 8192, 1024, 4
DK = DV = 256
TS = 512
NSEG = L // TS
C = 128
NCH = TS // C
FT = 256
KT = H // 128

BF = ml_dtypes.bfloat16


class Bf:
    __slots__ = ("w", "r", "x")
    def __init__(s, x=False):
        s.w = -1
        s.r = []
        s.x = x


class Prog:
    def __init__(s):
        s.ops = []

    def op(s, eng, fn, r=(), w=(), dma=False):
        rr = [b for b in r if not b.x]
        ww = list(w) + [b for b in r if b.x]
        deps = set()
        for b in rr:
            if b.w >= 0:
                deps.add(b.w)
        for b in ww:
            if b.w >= 0:
                deps.add(b.w)
            deps.update(b.r)
        i = len(s.ops)
        s.ops.append((eng, fn, deps, 16 if dma else 1))
        for b in rr:
            b.r.append(i)
        for b in ww:
            b.w = i
            b.r = []
        return i


def build_nc(nseg=NSEG):
    L_ = nseg * TS
    nc = bass.Bass("TRN2", target_bir_lowering=False)
    P = Prog()
    op = P.op

    def din(name, shape, dt=F32):
        return nc.dram_tensor(name, shape, dt, kind="ExternalInput").ap()

    xT = din("xT", [H, L_])
    Wq = din("Wq", [H, H]); Wk = din("Wk", [H, H]); Wv = din("Wv", [H, H])
    Wb = din("Wb", [H, NH])
    Wf1 = din("Wf1", [H, 2 * H]); bf1 = din("bf1", [2 * H]); Wf2 = din("Wf2", [2 * H, 16]); bf2 = din("bf2", [16])
    Wo = din("Wo", [H, H]); rmsw = din("rmsw", [DV])
    fq = din("fq", [H, 4]); fk = din("fk", [H, 4]); fv = din("fv", [H, 4])
    fs = din("fs", [H, 7]); fl = din("fl", [H, 64])
    WfcT = din("WfcT", [H, 4 * H])
    WprojT = din("WprojT", [4 * H, H])
    ID_F = din("ID_F", [128, 128]); ID_B = din("ID_B", [128, 128], BF16)
    NEG_SL = din("NEG_SL", [128, 128]); CAUS_T = din("CAUS_T", [128, 128])
    ONEROW_F = din("ONEROW_F", [1, 128]); ONEROW_B = din("ONEROW_B", [1, 128], BF16)
    ONECOL_B = din("ONECOL_B", [128, 1], BF16)
    H4 = din("H4", [128, 16], BF16)
    IND4R = din("IND4R", [4, 512], BF16)
    outT = nc.dram_tensor("outT", [H, L_], F32, kind="ExternalOutput").ap()
    wf1_sc = nc.dram_tensor("wf1_sc", [128, 16384], BF16, kind="Internal").ap()
    wo_sc = nc.dram_tensor("wo_sc", [128, 8192], BF16, kind="Internal").ap()
    wfc_sc = nc.dram_tensor("wfc_sc", [128, 32768], BF16, kind="Internal").ap()
    wpj_sc = nc.dram_tensor("wpj_sc", [128, 32768], BF16, kind="Internal").ap()

    ctx = []

    def sb(name, shape, dt):
        t = nc.sbuf_tensor(name, shape, dt)
        ctx.append(t)
        return t.__enter__()

    def ps(name, shape, dt):
        t = nc.psum_tensor(name, shape, dt)
        ctx.append(t)
        return t.__enter__()

    wq_sb = sb("wq_sb", [128, 8192], BF16)
    wk_sb = sb("wk_sb", [128, 8192], BF16)
    wv_sb = sb("wv_sb", [128, 8192], BF16)
    wb_sb = sb("wb_sb", [128, 32], BF16)
    wf2_sb = sb("wf2_sb", [128, 256], BF16)
    bf1T = sb("bf1T", [128, 16], F32)
    bf2_sb = sb("bf2_sb", [16, 1], F32)
    rmsw_bc = sb("rmsw_bc", [128, DV], BF16)
    fq_sb = sb("fq_sb", [128, 32], F32)
    fk_sb = sb("fk_sb", [128, 32], F32)
    fv_sb = sb("fv_sb", [128, 32], F32)
    fs_sb = sb("fs_sb", [128, 56], F32)
    fl_sb = sb("fl_sb", [128, 512], F32)
    id_f = sb("id_f", [128, 128], F32)
    id_b = sb("id_b", [128, 128], BF16)
    neg_sl = sb("neg_sl", [128, 128], F32)
    caus_t = sb("caus_t", [128, 128], F32)
    onerow_f = sb("onerow_f", [1, 128], F32)
    onerow_b = sb("onerow_b", [1, 128], BF16)
    onecol_b = sb("onecol_b", [128, 1], BF16)
    h4_sb = sb("h4_sb", [128, 16], BF16)
    ind4r = sb("ind4r", [4, 512], BF16)
    rms_row = sb("rms_row", [1, DV], BF16)

    rslot = [sb("rslot0", [128, 4096], BF16), sb("rslot1", [128, 4096], BF16)]

    xstage = [sb("xstage0", [128, 512], F32), sb("xstage1", [128, 512], F32)]
    x_bf = sb("x_bf", [128, 4096], BF16)
    xn = sb("xn", [128, 4096], BF16)
    xa = sb("xa", [128, 4096], BF16)
    lnrow = sb("lnrow", [1, 2048], BF16)
    cin = sb("cin", [128, 520], BF16)
    acc_cv = sb("acc_cv", [128, 1024], BF16)
    hq_h = sb("hq_h", [128, 24], BF16)
    hk_h = sb("hk_h", [128, 24], BF16)
    hv_h = sb("hv_h", [128, 24], BF16)
    q_buf = sb("q_buf", [128, 4096], BF16)
    k_buf = sb("k_buf", [128, 4096], BF16)
    v_buf = sb("v_buf", [128, 8 * 575], BF16)
    lsll = sb("lsll", [128, 8192], BF16)
    statA = sb("statA", [64, TS], BF16)
    statB = sb("statB", [64, TS], BF16)
    rowt1 = sb("rowt1", [4, TS], F32)
    rowt2 = sb("rowt2", [4, TS], F32)
    statA_tm = sb("statA_tm", [128, 4 * 64], F32)
    statB_tm = sb("statB_tm", [128, 4 * 64], F32)
    q2s = [sb("q2s0", [128, 512], BF16), sb("q2s1", [128, 512], BF16)]
    kn_fm = sb("kn_fm", [128, 256], BF16)
    kn_tm = sb("kn_tm", [128, 256], BF16)
    v_tm = sb("v_tm", [128, 256], BF16)
    ls_tm = sb("ls_tm", [128, 256], BF16)
    ll_tm = sb("ll_tm", [128, 256], BF16)
    L_sb = sb("L_sb", [128, 128], BF16)
    M_sb = sb("M_sb", [128, 128], BF16)
    Xb = [sb("Xb0", [128, 128], BF16), sb("Xb1", [128, 128], BF16)]
    Yb = [sb("Yb0", [128, 128], BF16), sb("Yb1", [128, 128], BF16)]
    Pb = [sb("Pb0", [128, 128], BF16), sb("Pb1", [128, 128], BF16)]
    Tn_sb = sb("Tn_sb", [128, 128], BF16)
    at_sb = sb("at_sb", [128, 128], BF16)
    wt_sb = sb("wt_sb", [128, 256], BF16)
    g_sb = sb("g_sb", [128, 256], BF16)
    S_bf = [sb(f"S_bf{h}", [128, 512], BF16) for h in range(NH)]
    gt = [sb("gt0", [128, 256], BF16), sb("gt1", [128, 256], BF16)]
    sqs = sb("sqs", [128, 256], BF16)
    colf = sb("colf", [128, 16], F32)
    onT = sb("onT", [128, 4096], BF16)
    memb = [sb("memb0", [128, FT], BF16), sb("memb1", [128, FT], BF16)]
    yb_sb = sb("yb_sb", [128, FT], BF16)
    zb = sb("zb", [128, FT], BF16)
    accb = sb("accb", [128, FT], BF16)
    ostage = [sb("ostage0", [128, FT], F32), sb("ostage1", [128, FT], F32)]

    S_ps = [ps(f"S_ps{h}", [128, 512], F32) for h in range(NH)]
    gps = ps("gps", [128, 512], F32)
    work = ps("work", [128, 1024], BF16)
    pj = [ps("pj0", [128, 512], F32), ps("pj1", [128, 512], F32)]

    B = {}

    def nb(name):
        B[name] = Bf()
        return B[name]

    for n in ("wq wk wv wb wf2 bf1T bf2 rmswbc fq fk fv fs fl consts rmsrow "
              "x_bf xn xa lnrow cin acc hq hk hv qb kb vb ls ll statA statB "
              "rowt1 rowt2 statAtm statBtm knfm kntm vtm lstm lltm Lsb Msb "
              "Tn atsb wtsb gsb sqs colf onT zb accb gps_lo gps_hi").split():
        nb(n)
    B["gps_lo"] = B["gps_hi"] = Bf(x=True)
    xstB = [nb("xst0"), nb("xst1")]
    q2B = [nb("q2s0"), nb("q2s1")]
    XB = [nb("Xb0"), nb("Xb1")]
    YB = [nb("Yb0"), nb("Yb1")]
    PB = [nb("Pb0"), nb("Pb1")]
    SbfB = [nb(f"Sbf{h}") for h in range(NH)]
    SpsB = [Bf(x=True) for h in range(NH)]
    gtB = [nb("gt0"), nb("gt1")]
    ybB = nb("yb")
    membB = [nb("memb0"), nb("memb1")]
    ostB = [nb("ost0"), nb("ost1")]
    rslB = [nb("rsl0"), nb("rsl1")]
    _pj0, _pj1, _wk = Bf(x=True), Bf(x=True), Bf(x=True)
    pjB = [[_pj0] * 4, [_pj1] * 4]
    wkB = [_wk] * 8
    scB = nb("scratch_dram")
    outB = nb("outB")

    def mm(out_ap, lhsT, rhs, r, w, start=True, stop=True):
        return op("pe", lambda e, o=out_ap, a=lhsT, b=rhs, s1=start, s2=stop:
                  e.matmul(o, a, b, start=s1, stop=s2, skip_group_check=True), r=r, w=w)

    def tp(out_ap, in_ap, ident, r, w):
        return op("pe", lambda e, o=out_ap, i=in_ap, d=ident: e.transpose(o, i, d), r=r, w=w)

    def acp(out_ap, in_ap, r, w, scale=None, func=AF.Copy, bias=0.0):
        if scale is None:
            return op("act", lambda e, o=out_ap, i=in_ap: e.copy(o, i), r=r, w=w)
        return op("act", lambda e, o=out_ap, i=in_ap, s=scale, f=func, bb=bias:
                  e.activation(o, i, f, bias=bb, scale=s), r=r, w=w)

    def act(out_ap, in_ap, func, r, w, bias=0.0, scale=1.0):
        return op("act", lambda e, o=out_ap, i=in_ap, f=func, bb=bias, s=scale:
                  e.activation(o, i, f, bias=bb, scale=s), r=r, w=w)

    def tt(out_ap, a, b, alu, r, w):
        return op("dve", lambda e, o=out_ap, x=a, y=b, u=alu: e.tensor_tensor(o, x, y, u), r=r, w=w)

    def ts(out_ap, a, s1, o0, r, w, s2=None, o1=ALU.bypass, accum=None):
        return op("dve", lambda e, o=out_ap, x=a, a1=s1, a2=s2, u0=o0, u1=o1, ac=accum:
                  e.tensor_scalar(o, x, a1, a2, u0, u1, accum_out=ac), r=r, w=w)

    def stt(out_ap, in0, s, in1, o0, o1, r, w, accum=None):
        return op("dve", lambda e, o=out_ap, x=in0, a1=s, y=in1, u0=o0, u1=o1, ac=accum:
                  e.scalar_tensor_tensor(o, x, a1, y, u0, u1, accum_out=ac), r=r, w=w)

    def dma(eng, out_ap, in_ap, r, w):
        return op(eng, lambda e, o=out_ap, i=in_ap: e.dma_start(o, i), r=r, w=w, dma=True)

    CB = B["consts"]

    dma("sin", id_f[:, :], ID_F, r=(), w=(CB,))
    dma("sin", id_b[:, :], ID_B, r=(), w=(CB,))
    dma("sin", neg_sl[:, :], NEG_SL, r=(), w=(CB,))
    dma("sin", caus_t[:, :], CAUS_T, r=(), w=(CB,))
    dma("sin", onerow_f[:, :], ONEROW_F, r=(), w=(CB,))
    dma("sin", onerow_b[:, :], ONEROW_B, r=(), w=(CB,))
    dma("sin", onecol_b[:, :], ONECOL_B, r=(), w=(CB,))
    dma("sin", h4_sb[:, :], H4, r=(), w=(CB,))
    dma("sin", ind4r[:, :], IND4R, r=(), w=(CB,))
    dma("sin", bf2_sb[:, :], bf2.rearrange("(a b) -> a b", b=1), r=(), w=(B["bf2"],))
    for kt in range(KT):
        dma("sin", fq_sb[:, kt * 4:(kt + 1) * 4], fq[kt * 128:(kt + 1) * 128, :], r=(), w=(B["fq"],))
        dma("sin", fk_sb[:, kt * 4:(kt + 1) * 4], fk[kt * 128:(kt + 1) * 128, :], r=(), w=(B["fk"],))
        dma("sin", fv_sb[:, kt * 4:(kt + 1) * 4], fv[kt * 128:(kt + 1) * 128, :], r=(), w=(B["fv"],))
        dma("sin", fs_sb[:, kt * 7:(kt + 1) * 7], fs[kt * 128:(kt + 1) * 128, :], r=(), w=(B["fs"],))
        dma("sin", fl_sb[:, kt * 64:(kt + 1) * 64], fl[kt * 128:(kt + 1) * 128, :], r=(), w=(B["fl"],))
    op("dve", lambda e: e.memset(v_buf[:, :], 0.0), w=(B["vb"],))
    op("dve", lambda e: e.memset(hq_h[:, :], 0.0), w=(B["hq"],))
    op("dve", lambda e: e.memset(hk_h[:, :], 0.0), w=(B["hk"],))
    op("dve", lambda e: e.memset(hv_h[:, :], 0.0), w=(B["hv"],))

    pp = [0]

    def conv_chunk(src_ap, dst_sbuf_ap, wbuf, n):
        i = pp[0] % 2
        pp[0] += 1
        dma("sin", xstage[i][:, 0:n], src_ap, r=(), w=(xstB[i],))
        acp(dst_sbuf_ap, xstage[i][:, 0:n], r=(xstB[i], CB), w=(wbuf,))

    def conv_chunk_dram(src_ap, dst_dram_ap, n):
        i = pp[0] % 2
        pp[0] += 1
        dma("sin", xstage[i][:, 0:n], src_ap, r=(), w=(xstB[i],))
        j = pp[0] % 2
        pp[0] += 1
        acp(q2s[j][:, 0:n], xstage[i][:, 0:n], r=(xstB[i],), w=(q2B[j],))
        dma("sin", dst_dram_ap, q2s[j][:, 0:n], r=(q2B[j],), w=(scB,))

    for (Wsrc, wdst, wbuf) in ((Wq, wq_sb, B["wq"]), (Wk, wk_sb, B["wk"]), (Wv, wv_sb, B["wv"])):
        for kt in range(KT):
            for hh in range(2):
                conv_chunk(Wsrc[kt * 128:(kt + 1) * 128, hh * 512:(hh + 1) * 512],
                           wdst[:, kt * 1024 + hh * 512: kt * 1024 + (hh + 1) * 512], wbuf, 512)
    for kt in range(KT):
        dma("sin", xstage[0][:, kt * 4:(kt + 1) * 4], Wb[kt * 128:(kt + 1) * 128, :], r=(B["wb"],), w=(xstB[0],))
    acp(wb_sb[:, :], xstage[0][:, 0:32], r=(xstB[0],), w=(B["wb"],))
    for kt in range(16):
        dma("sin", xstage[1][:, kt * 16:(kt + 1) * 16], Wf2[kt * 128:(kt + 1) * 128, :], r=(B["wf2"],), w=(xstB[1],))
    acp(wf2_sb[:, :], xstage[1][:, 0:256], r=(xstB[1],), w=(B["wf2"],))
    dma("sin", xstage[0][0:16, 0:128], bf1.rearrange("(a b) -> a b", b=128), r=(xstB[0],), w=(xstB[0],))
    tp(pj[0][0:128, 0:16], xstage[0][0:16, 0:128], id_f[0:16, 0:16], r=(xstB[0], CB), w=(pjB[0][0],))
    acp(bf1T[:, :], pj[0][0:128, 0:16], r=(pjB[0][0],), w=(B["bf1T"],))
    dma("sin", xstage[0][0:1, 0:DV], rmsw.rearrange("(a b) -> a b", a=1), r=(xstB[0],), w=(xstB[0],))
    acp(rms_row[0:1, :], xstage[0][0:1, 0:DV], r=(xstB[0],), w=(B["rmsrow"],))
    mm(pj[1][:, 0:DV], onerow_b[0:1, :], rms_row[0:1, :], r=(CB, B["rmsrow"]), w=(pjB[1][0], pjB[1][1]))
    acp(rmsw_bc[:, :], pj[1][:, 0:DV], r=(pjB[1][0], pjB[1][1]), w=(B["rmswbc"],))
    for p in range(4):
        for kt in range(KT):
            conv_chunk_dram(Wf1[kt * 128:(kt + 1) * 128, p * 512:(p + 1) * 512],
                            wf1_sc[:, p * 4096 + kt * 512: p * 4096 + (kt + 1) * 512], 512)
    for p in range(2):
        for kt in range(KT):
            conv_chunk_dram(Wo[kt * 128:(kt + 1) * 128, p * 512:(p + 1) * 512],
                            wo_sc[:, p * 4096 + kt * 512: p * 4096 + (kt + 1) * 512], 512)
    for p in range(8):
        for kt in range(KT):
            conv_chunk_dram(WfcT[kt * 128:(kt + 1) * 128, p * 512:(p + 1) * 512],
                            wfc_sc[:, p * 4096 + kt * 512: p * 4096 + (kt + 1) * 512], 512)
    for p in range(8):
        for kt in range(32):
            conv_chunk_dram(WprojT[kt * 128:(kt + 1) * 128, p * 128:(p + 1) * 128],
                            wpj_sc[:, p * 4096 + kt * 128: p * 4096 + (kt + 1) * 128], 128)

    gcount = [0]

    LASTGC = nseg * NCH - 1
    for seg in range(nseg):
        s0 = seg * TS

        for mt in range(KT):
            i = mt % 2
            dma("sin", xstage[i][:, :], xT[mt * 128:(mt + 1) * 128, s0:s0 + TS], r=(), w=(xstB[i],))
            acp(x_bf[:, mt * 512:(mt + 1) * 512], xstage[i][:, :], r=(xstB[i],), w=(B["x_bf"],))

        pieces = [("wf1", 0, 0), ("wf1", 1, 0), ("wf1", 2, 0), ("wf1", 3, 0),
                  ("wo", 0, 0), ("wo", 1, 0)]
        for _th in range(TS // FT):
            pieces += [("wfc", p, _th) for p in range(8)] + [("wpj", p, _th) for p in range(8)]
        scmap = {"wf1": wf1_sc, "wo": wo_sc, "wfc": wfc_sc, "wpj": wpj_sc}
        pidx = {tp: i for i, tp in enumerate(pieces)}
        ploaded = [0]

        def pload(j):
            while ploaded[0] <= j:
                jj = ploaded[0]
                tag, p, _t = pieces[jj]
                i = jj % 2
                dma("sin", rslot[i][:, :], scmap[tag][:, p * 4096:(p + 1) * 4096],
                    r=(scB,), w=(rslB[i],))
                ploaded[0] = jj + 1

        def pget(tag, p, th=0):
            j = pidx[(tag, p, th)]
            pload(min(j + 1, len(pieces) - 1))
            return j % 2

        pload(0)

        for kt in range(KT):
            i = kt % 2
            act(q2s[i][:, :], x_bf[:, kt * 512:(kt + 1) * 512], AF.Square, r=(B["x_bf"],), w=(q2B[i],))
            mm(pj[0][0:1, 0:TS], onecol_b[:, 0:1], x_bf[:, kt * 512:(kt + 1) * 512],
               r=(CB, B["x_bf"]), w=tuple(pjB[0]), start=(kt == 0), stop=(kt == KT - 1))
            mm(pj[1][0:1, 0:TS], onecol_b[:, 0:1], q2s[i][:, :],
               r=(CB, q2B[i]), w=tuple(pjB[1]), start=(kt == 0), stop=(kt == KT - 1))
        ts(lnrow[0:1, 0:512], pj[0][0:1, 0:TS], 1.0 / H, ALU.mult, r=tuple(pjB[0]), w=(B["lnrow"],))
        ts(lnrow[0:1, 512:1024], pj[1][0:1, 0:TS], 1.0 / H, ALU.mult, r=tuple(pjB[1]), w=(B["lnrow"],))
        tt(lnrow[0:1, 1024:1536], lnrow[0:1, 0:512], lnrow[0:1, 0:512], ALU.mult, r=(B["lnrow"],), w=(B["lnrow"],))
        stt(lnrow[0:1, 1536:2048], lnrow[0:1, 512:1024], 1e-5, lnrow[0:1, 1024:1536], ALU.add, ALU.subtract,
            r=(B["lnrow"],), w=(B["lnrow"],))
        act(lnrow[0:1, 1024:1536], lnrow[0:1, 1536:2048], AF.Ln, r=(B["lnrow"],), w=(B["lnrow"],))
        ts(lnrow[0:1, 1536:2048], lnrow[0:1, 1024:1536], -0.5, ALU.mult, r=(B["lnrow"],), w=(B["lnrow"],))
        act(lnrow[0:1, 1024:1536], lnrow[0:1, 1536:2048], AF.Exp, r=(B["lnrow"],), w=(B["lnrow"],))
        mm(pj[0][:, 0:TS], onerow_b[0:1, :], lnrow[0:1, 0:512], r=(CB, B["lnrow"]), w=tuple(pjB[0]))
        mm(pj[1][:, 0:TS], onerow_b[0:1, :], lnrow[0:1, 1024:1536], r=(CB, B["lnrow"]), w=tuple(pjB[1]))
        acp(q2s[0][:, :], pj[0][:, 0:TS], r=tuple(pjB[0]), w=(q2B[0],))
        acp(q2s[1][:, :], pj[1][:, 0:TS], r=tuple(pjB[1]), w=(q2B[1],))
        for kt in range(KT):
            tt(xn[:, kt * 512:(kt + 1) * 512], x_bf[:, kt * 512:(kt + 1) * 512], q2s[0][:, :], ALU.subtract,
               r=(B["x_bf"], q2B[0]), w=(B["xn"],))
            tt(xn[:, kt * 512:(kt + 1) * 512], xn[:, kt * 512:(kt + 1) * 512], q2s[1][:, :], ALU.mult,
               r=(B["xn"], q2B[1]), w=(B["xn"],))

        for kt in range(KT):
            mm(pj[0][0:4, 0:TS], wb_sb[:, kt * 4:(kt + 1) * 4], xn[:, kt * 512:(kt + 1) * 512],
               r=(B["wb"], B["xn"]), w=tuple(pjB[0]), start=(kt == 0), stop=(kt == KT - 1))
        act(statA[0:4, :], pj[0][0:4, 0:TS], AF.Sigmoid, r=tuple(pjB[0]), w=(B["statA"],))

        for (wt, wbuf, fsb, fB, hb, hB, obuf, oB) in (
                (wq_sb, B["wq"], fq_sb, B["fq"], hq_h, B["hq"], q_buf, B["qb"]),
                (wk_sb, B["wk"], fk_sb, B["fk"], hk_h, B["hk"], k_buf, B["kb"]),
                (wv_sb, B["wv"], fv_sb, B["fv"], hv_h, B["hv"], v_buf, B["vb"])):
            for mt in range(KT):
                bank = mt % 2
                for kt in range(KT):
                    mm(pj[bank][:, 0:TS], wt[:, kt * 1024 + mt * 128: kt * 1024 + (mt + 1) * 128],
                       xn[:, kt * 512:(kt + 1) * 512], r=(wbuf, B["xn"]), w=tuple(pjB[bank]),
                       start=(kt == 0), stop=(kt == KT - 1))
                acp(cin[:, 0:3], hb[:, mt * 3:(mt + 1) * 3], r=(hB,), w=(B["cin"],))
                acp(cin[:, 3:515], pj[bank][:, 0:TS], r=tuple(pjB[bank]), w=(B["cin"],))
                acp(hb[:, mt * 3:(mt + 1) * 3], cin[:, 512:515], r=(B["cin"],), w=(hB,))
                ts(acc_cv[:, 0:512], cin[:, 0:512], fsb[:, mt * 4:mt * 4 + 1], ALU.mult,
                   r=(B["cin"], fB), w=(B["acc"],))
                for j in range(1, 4):
                    src = acc_cv[:, 0:512] if j % 2 == 1 else acc_cv[:, 512:1024]
                    dst = acc_cv[:, 512:1024] if j % 2 == 1 else acc_cv[:, 0:512]
                    stt(dst, cin[:, j:j + 512], fsb[:, mt * 4 + j:mt * 4 + j + 1], src, ALU.mult, ALU.add,
                        r=(B["cin"], fB, B["acc"]), w=(B["acc"],))
                if obuf is v_buf:
                    acp(v_buf[:, mt * 575:mt * 575 + 63], v_buf[:, mt * 575 + 512:mt * 575 + 575],
                        r=(oB,), w=(oB,))
                    act(v_buf[:, mt * 575 + 63:(mt + 1) * 575], acc_cv[:, 512:1024], AF.Silu,
                        r=(B["acc"],), w=(oB,))
                else:
                    act(obuf[:, mt * 512:(mt + 1) * 512], acc_cv[:, 512:1024], AF.Silu,
                        r=(B["acc"],), w=(oB,))

        for (srcb, srcB_, off, rows, negd) in ((q_buf, B["qb"], 0, statB[32:36, :], True),
                                               (k_buf, B["kb"], 0, statB[0:4, :], False)):
            for kt in range(KT):
                i = kt % 2
                act(q2s[i][:, :], srcb[:, kt * 512:(kt + 1) * 512], AF.Square, r=(srcB_,), w=(q2B[i],))
                mm(pj[0][0:4, 0:TS], h4_sb[:, 4 * (kt // 2):4 * (kt // 2) + 4], q2s[i][:, :],
                   r=(CB, q2B[i]), w=tuple(pjB[0]), start=(kt == 0), stop=(kt == KT - 1))
            act(rowt1[0:4, :], pj[0][0:4, 0:TS], AF.Ln, r=tuple(pjB[0]), w=(B["rowt1"],))
            ts(rowt2[0:4, :], rowt1[0:4, :], -0.5, ALU.mult, r=(B["rowt1"],), w=(B["rowt2"],))
            if negd:
                act(rowt1[0:4, :], rowt2[0:4, :], AF.Exp, r=(B["rowt2"],), w=(B["rowt1"],))
                ts(rows, rowt1[0:4, :], -1.0, ALU.mult, r=(B["rowt1"],), w=(B["statB"],))
            else:
                act(rows, rowt2[0:4, :], AF.Exp, r=(B["rowt2"],), w=(B["statB"],))

        for mt in range(16):
            bank = mt % 2
            p = mt // 4
            sl = pget("wf1", p)
            for kt in range(KT):
                mm(pj[bank][:, 0:TS], rslot[sl][:, kt * 512 + (mt % 4) * 128: kt * 512 + (mt % 4 + 1) * 128],
                   xn[:, kt * 512:(kt + 1) * 512], r=(rslB[sl], B["xn"]), w=tuple(pjB[bank]),
                   start=(kt == 0), stop=(kt == KT - 1))
            i = mt % 2
            act(q2s[i][:, :], pj[bank][:, 0:TS], AF.Gelu, r=tuple(pjB[bank]), w=(q2B[i],),
                bias=bf1T[:, mt:mt + 1])
            mm(gps[0:16, 0:TS], wf2_sb[:, mt * 16:(mt + 1) * 16], q2s[i][:, :],
               r=(B["wf2"], q2B[i], B["bf1T"]), w=(B["gps_lo"], B["gps_hi"]),
               start=(mt == 0), stop=(mt == 15))
        act(statA[32:48, :], gps[0:16, 0:TS], AF.Exp, r=(B["gps_lo"], B["gps_hi"]), w=(B["statA"],),
            bias=bf2_sb[0:16, 0:1])

        for h in range(NH):
            for mt in (2 * h, 2 * h + 1):
                base = mt * 575
                for (nt, fsb2, fB2, outsl) in ((7, fs_sb, B["fs"], lsll[:, mt * 512:(mt + 1) * 512]),
                                               (64, fl_sb, B["fl"], lsll[:, 4096 + mt * 512:4096 + (mt + 1) * 512])):
                    sh0 = 57 if nt == 7 else 0
                    ts(acc_cv[:, 0:512], v_buf[:, base + sh0:base + sh0 + 512],
                       fsb2[:, mt * nt:mt * nt + 1], ALU.mult, r=(B["vb"], fB2), w=(B["acc"],))
                    for j in range(1, nt):
                        src = acc_cv[:, 0:512] if j % 2 == 1 else acc_cv[:, 512:1024]
                        dst = acc_cv[:, 512:1024] if j % 2 == 1 else acc_cv[:, 0:512]
                        if j == nt - 1:
                            dst = outsl
                        wb_ = (B["ls"] if nt == 7 else B["ll"],) if j == nt - 1 else (B["acc"],)
                        stt(dst, v_buf[:, base + sh0 + j:base + sh0 + j + 512],
                            fsb2[:, mt * nt + j:mt * nt + j + 1], src, ALU.mult, ALU.add,
                            r=(B["vb"], fB2, B["acc"]), w=wb_)

            for c in range(NCH):
                gc = gcount[0] + c
                t0 = c * 128
                if h == 0:
                    tp(work[:, 0:64], statA[0:64, t0:t0 + 128], id_b[0:64, 0:64],
                       r=(B["statA"], CB), w=(wkB[0],))
                    acp(statA_tm[:, c * 64:(c + 1) * 64], work[:, 0:64], r=(wkB[0],), w=(B["statAtm"],))
                    tp(work[:, 128:192], statB[0:64, t0:t0 + 128], id_b[0:64, 0:64],
                       r=(B["statB"], CB), w=(wkB[1],))
                    acp(statB_tm[:, c * 64:(c + 1) * 64], work[:, 128:192], r=(wkB[1],), w=(B["statBtm"],))
                beta_c = statA_tm[:, c * 64 + h:c * 64 + h + 1]
                rk_bc_r = (B["statBtm"],)
                mm(pj[1][:, 384:512], ind4r[:, h * 128:(h + 1) * 128], statB[0:4, t0:t0 + 128],
                   r=(CB, B["statB"]), w=(pjB[1][3],))
                for d in range(2):
                    tt(kn_fm[:, d * 128:(d + 1) * 128], k_buf[:, (2 * h + d) * 512 + t0:(2 * h + d) * 512 + t0 + 128],
                       pj[1][:, 384:512], ALU.mult, r=(B["kb"], pjB[1][3]), w=(B["knfm"],))
                mm(pj[0][:, 384:512], kn_fm[:, 0:128], kn_fm[:, 0:128], r=(B["knfm"],), w=(pjB[0][3],),
                   start=True, stop=False)
                mm(pj[0][:, 384:512], kn_fm[:, 128:256], kn_fm[:, 128:256], r=(B["knfm"],), w=(pjB[0][3],),
                   start=False, stop=True)
                stt(L_sb[:, :], pj[0][:, 384:512], beta_c, neg_sl[:, :], ALU.mult, ALU.mult,
                    r=(pjB[0][3], B["statAtm"], CB), w=(B["Lsb"],))
                tp(work[:, 256:384], L_sb[:, :], id_b[:, :], r=(B["Lsb"], CB), w=(wkB[2],))
                acp(M_sb[:, :], work[:, 256:384], r=(wkB[2],), w=(B["Msb"],))
                tt(Pb[0][:, :], M_sb[:, :], id_b[:, :], ALU.add, r=(B["Msb"], CB), w=(PB[0],))
                Xc, Yc, XcB, YcB = L_sb, M_sb, B["Lsb"], B["Msb"]
                for j in range(6):
                    nx, ny = Xb[j % 2], Yb[j % 2]
                    nxB, nyB = XB[j % 2], YB[j % 2]
                    mm(pj[0][:, 0:128], Yc[:, :], Xc[:, :], r=(YcB, XcB), w=(pjB[0][0],))
                    acp(nx[:, :], pj[0][:, 0:128], r=(pjB[0][0],), w=(nxB,))
                    if j < 5:
                        mm(pj[1][:, 0:128], Xc[:, :], Yc[:, :], r=(XcB, YcB), w=(pjB[1][0],))
                        acp(ny[:, :], pj[1][:, 0:128], r=(pjB[1][0],), w=(nyB,))
                    pbank = pj[(j + 1) % 2]
                    pbB = pjB[(j + 1) % 2][0]
                    mm(pbank[:, 256:384], nx[:, :], Pb[j % 2][:, :], r=(nxB, PB[j % 2]), w=(pbB,))
                    tt(Pb[(j + 1) % 2][:, :], pbank[:, 256:384], Pb[j % 2][:, :], ALU.add,
                       r=(pbB, PB[j % 2]), w=(PB[(j + 1) % 2],))
                    Xc, Yc, XcB, YcB = nx, ny, nxB, nyB
                TT_, TTB = Pb[0], PB[0]
                ts(Tn_sb[:, :], TT_[:, :], beta_c, ALU.mult, r=(TTB, B["statAtm"]), w=(B["Tn"],),
                   s2=-1.0, o1=ALU.mult)
                for d in range(2):
                    mm(pj[1][:, 0:128], kn_fm[:, d * 128:(d + 1) * 128],
                       q_buf[:, (2 * h + d) * 512 + t0:(2 * h + d) * 512 + t0 + 128],
                       r=(B["knfm"], B["qb"]), w=(pjB[1][0],), start=(d == 0), stop=(d == 1))
                tt(at_sb[:, :], pj[1][:, 0:128], caus_t[:, :], ALU.mult, r=(pjB[1][0], CB), w=(B["atsb"],))
                for d in range(2):
                    tp(work[:, (4 + d) * 128:(5 + d) * 128], kn_fm[:, d * 128:(d + 1) * 128], id_b[:, :],
                       r=(B["knfm"], CB), w=(wkB[4 + d],))
                    acp(kn_tm[:, d * 128:(d + 1) * 128], work[:, (4 + d) * 128:(5 + d) * 128],
                        r=(wkB[4 + d],), w=(B["kntm"],))
                    tp(work[:, (6 + d) * 128:(7 + d) * 128],
                       v_buf[:, (2 * h + d) * 575 + 63 + t0:(2 * h + d) * 575 + 63 + t0 + 128], id_b[:, :],
                       r=(B["vb"], CB), w=(wkB[6 + d],))
                    acp(v_tm[:, d * 128:(d + 1) * 128], work[:, (6 + d) * 128:(7 + d) * 128],
                        r=(wkB[6 + d],), w=(B["vtm"],))
                for d in range(2):
                    mm(pj[1][:, 128 + d * 128:256 + d * 128], kn_tm[:, d * 128:(d + 1) * 128], Tn_sb[:, :],
                       r=(B["kntm"], B["Tn"]), w=(pjB[1][1 + d],))
                    acp(wt_sb[:, d * 128:(d + 1) * 128], pj[1][:, 128 + d * 128:256 + d * 128],
                        r=(pjB[1][1 + d],), w=(B["wtsb"],))
                if gc > 0:
                    for d in range(2):
                        mm(gps[:, 0:256], wt_sb[:, d * 128:(d + 1) * 128], S_bf[h][:, d * 256:(d + 1) * 256],
                           r=(B["wtsb"], SbfB[h]), w=(B["gps_lo"],), start=(d == 0), stop=False)
                    mm(gps[:, 0:256], Tn_sb[:, :], v_tm[:, :], r=(B["Tn"], B["vtm"]), w=(B["gps_lo"],),
                       start=False, stop=True)
                else:
                    mm(gps[:, 0:256], Tn_sb[:, :], v_tm[:, :], r=(B["Tn"], B["vtm"]), w=(B["gps_lo"],))
                acp(g_sb[:, :], gps[:, 0:256], r=(B["gps_lo"],), w=(B["gsb"],))
                if gc > 0:
                    for d in range(2):
                        mm(gps[:, 256:512], q_buf[:, (2 * h + d) * 512 + t0:(2 * h + d) * 512 + t0 + 128],
                           S_bf[h][:, d * 256:(d + 1) * 256], r=(B["qb"], SbfB[h]), w=(B["gps_hi"],),
                           start=(d == 0), stop=False)
                    mm(gps[:, 256:512], at_sb[:, :], g_sb[:, :], r=(B["atsb"], B["gsb"]), w=(B["gps_hi"],),
                       start=False, stop=True)
                else:
                    mm(gps[:, 256:512], at_sb[:, :], g_sb[:, :], r=(B["atsb"], B["gsb"]), w=(B["gps_hi"],))
                for d in range(2):
                    mm(S_ps[h][:, d * 256:(d + 1) * 256], kn_tm[:, d * 128:(d + 1) * 128], g_sb[:, :],
                       r=(B["kntm"], B["gsb"]), w=(SpsB[h],), start=(gc == 0 and d == 0), stop=True)
                if gc < LASTGC:
                    acp(S_bf[h][:, :], S_ps[h][:, :], r=(SpsB[h],), w=(SbfB[h],))
                for d in range(2):
                    tp(work[:, d * 128:(d + 1) * 128],
                       lsll[:, (2 * h + d) * 512 + t0:(2 * h + d) * 512 + t0 + 128], id_b[:, :],
                       r=(B["ls"], CB), w=(wkB[d],))
                    acp(ls_tm[:, d * 128:(d + 1) * 128], work[:, d * 128:(d + 1) * 128],
                        r=(wkB[d],), w=(B["lstm"],))
                    tp(work[:, (2 + d) * 128:(3 + d) * 128],
                       lsll[:, 4096 + (2 * h + d) * 512 + t0:4096 + (2 * h + d) * 512 + t0 + 128], id_b[:, :],
                       r=(B["ll"], CB), w=(wkB[2 + d],))
                    acp(ll_tm[:, d * 128:(d + 1) * 128], work[:, (2 + d) * 128:(3 + d) * 128],
                        r=(wkB[2 + d],), w=(B["lltm"],))
                e_ = [statA_tm[:, c * 64 + 32 + 4 * h + i:c * 64 + 32 + 4 * h + i + 1] for i in range(4)]
                nrq = statB_tm[:, c * 64 + 32 + h:c * 64 + 32 + h + 1]
                tt(colf[:, 0:1], e_[0], e_[1], ALU.add, r=(B["statAtm"],), w=(B["colf"],))
                tt(colf[:, 1:2], e_[2], e_[3], ALU.add, r=(B["statAtm"],), w=(B["colf"],))
                tt(colf[:, 2:3], colf[:, 0:1], colf[:, 1:2], ALU.add, r=(B["colf"],), w=(B["colf"],))
                op("dve", lambda e, o=colf[:, 3:4], i=colf[:, 2:3]: e.reciprocal(o, i),
                   r=(B["colf"],), w=(B["colf"],))
                r_c = colf[:, 3:4]
                ts(colf[:, 4:5], e_[0], r_c, ALU.mult, r=(B["statAtm"], B["colf"]), w=(B["colf"],))
                ts(colf[:, 5:6], e_[1], r_c, ALU.mult, r=(B["statAtm"], B["colf"]), w=(B["colf"],))
                ts(colf[:, 6:7], e_[2], r_c, ALU.mult, r=(B["statAtm"], B["colf"]), w=(B["colf"],),
                   s2=nrq, o1=ALU.mult)
                ts(colf[:, 7:8], e_[3], r_c, ALU.mult, r=(B["statAtm"], B["colf"]), w=(B["colf"],))
                ts(gt[0][:, :], ls_tm[:, :], colf[:, 4:5], ALU.mult, r=(B["lstm"], B["colf"]), w=(gtB[0],))
                stt(gt[1][:, :], ll_tm[:, :], colf[:, 5:6], gt[0][:, :], ALU.mult, ALU.add,
                    r=(B["lltm"], B["colf"], gtB[0]), w=(gtB[1],))
                stt(gt[0][:, :], gps[:, 256:512], colf[:, 6:7], gt[1][:, :], ALU.mult, ALU.add,
                    r=(B["gps_hi"], B["colf"], gtB[1]), w=(gtB[0],))
                stt(gt[1][:, :], v_tm[:, :], colf[:, 7:8], gt[0][:, :], ALU.mult, ALU.add,
                    r=(B["vtm"], B["colf"], gtB[0]), w=(gtB[1],))
                stt(sqs[:, :], gt[1][:, :], 1.0, gt[1][:, :], ALU.mult, ALU.mult,
                    r=(gtB[1],), w=(B["sqs"], B["colf"]), accum=colf[:, 8:9])
                ts(colf[:, 9:10], colf[:, 8:9], 1.0 / DV, ALU.mult, r=(B["colf"],), w=(B["colf"],),
                   s2=1e-5, o1=ALU.add)
                act(colf[:, 10:11], colf[:, 9:10], AF.Ln, r=(B["colf"],), w=(B["colf"],))
                ts(colf[:, 11:12], colf[:, 10:11], -0.5, ALU.mult, r=(B["colf"],), w=(B["colf"],))
                act(colf[:, 12:13], colf[:, 11:12], AF.Exp, r=(B["colf"],), w=(B["colf"],))
                stt(gt[0][:, :], gt[1][:, :], colf[:, 12:13], rmsw_bc[:, :], ALU.mult, ALU.mult,
                    r=(gtB[1], B["colf"], B["rmswbc"]), w=(gtB[0],))
                for d in range(2):
                    tp(work[:, (4 + d) * 128:(5 + d) * 128], gt[0][:, d * 128:(d + 1) * 128], id_b[:, :],
                       r=(gtB[0], CB), w=(wkB[4 + d],))
                    acp(onT[:, (2 * h + d) * 512 + t0:(2 * h + d) * 512 + t0 + 128],
                        work[:, (4 + d) * 128:(5 + d) * 128], r=(wkB[4 + d],), w=(B["onT"],))
        gcount[0] += NCH

        for mt in range(KT):
            bank = mt % 2
            p = mt // 4
            sl = pget("wo", p)
            for kt in range(KT):
                mm(pj[bank][:, 0:TS], rslot[sl][:, kt * 512 + (mt % 4) * 128: kt * 512 + (mt % 4 + 1) * 128],
                   onT[:, kt * 512:(kt + 1) * 512], r=(rslB[sl], B["onT"]), w=tuple(pjB[bank]),
                   start=(kt == 0), stop=(kt == KT - 1))
            tt(xa[:, mt * 512:(mt + 1) * 512], pj[bank][:, 0:TS], x_bf[:, mt * 512:(mt + 1) * 512],
               ALU.add, r=tuple(pjB[bank]) + (B["x_bf"],), w=(B["xa"],))

        for th in range(TS // FT):
            tof = th * FT
            for fm in range(32):
                bank = fm % 2
                p = fm // 4
                sl = pget("wfc", p, th)
                for kt in range(KT):
                    mm(pj[bank][:, 0:FT], rslot[sl][:, kt * 512 + (fm % 4) * 128: kt * 512 + (fm % 4 + 1) * 128],
                       xn[:, kt * 512 + tof: kt * 512 + tof + FT], r=(rslB[sl], B["xn"]),
                       w=tuple(pjB[bank]), start=(kt == 0), stop=(kt == KT - 1))
                i = fm % 2
                act(q2s[i][:, 0:FT], pj[bank][:, 0:FT], AF.Relu, r=tuple(pjB[bank]), w=(q2B[i],))
                tt(lsll[:, fm * 256:(fm + 1) * 256], q2s[i][:, 0:FT], q2s[i][:, 0:FT], ALU.mult,
                   r=(q2B[i],), w=(B["ls"], B["ll"]))
            for mt in range(KT):
                bank = mt % 2
                sl = pget("wpj", mt, th)
                for kt in range(32):
                    mm(pj[bank][:, 0:FT], rslot[sl][:, kt * 128:(kt + 1) * 128],
                       lsll[:, kt * 256:(kt + 1) * 256], r=(rslB[sl], B["ls"], B["ll"]),
                       w=tuple(pjB[bank]), start=(kt == 0), stop=(kt == 31))
                acp(yb_sb[:, :], pj[bank][:, 0:FT], r=tuple(pjB[bank]), w=(ybB,))
                ts(accb[:, :], yb_sb[:, :], 1.0, ALU.is_gt, r=(ybB,), w=(B["accb"],))
                prev_ap, prev_B = yb_sb, ybB
                for st in range(4):
                    stt(zb[:, :], prev_ap[:, :], 1.0, yb_sb[:, :], ALU.is_gt, ALU.subtract,
                        r=(prev_B, ybB), w=(B["zb"],))
                    stt(memb[st % 2][:, :], prev_ap[:, :], 0.9, zb[:, :], ALU.mult, ALU.subtract,
                        r=(prev_B, B["zb"]), w=(membB[st % 2],))
                    prev_ap, prev_B = memb[st % 2], membB[st % 2]
                    stt(accb[:, :], prev_ap[:, :], 1.0, accb[:, :], ALU.is_gt, ALU.add,
                        r=(prev_B, B["accb"]), w=(B["accb"],))
                i = mt % 2
                stt(ostage[i][:, :], accb[:, :], 0.2, xa[:, mt * 512 + tof: mt * 512 + tof + FT],
                    ALU.mult, ALU.add, r=(B["accb"], B["xa"]), w=(ostB[i],))
                dma("pool", outT[mt * 128:(mt + 1) * 128, s0 + tof:s0 + tof + FT], ostage[i][:, :],
                    r=(ostB[i],), w=(outB,))

    RDMA = 6
    ops = P.ops
    n = len(ops)
    SELF_SYNC = {"dve", "act", "pool"}
    need = [False] * n
    for k in range(n):
        e, _, deps, inc = ops[k]
        if inc == 16:
            need[k] = True
        for j in deps:
            if ops[j][0] != e or ops[j][0] in SELF_SYNC:
                need[j] = True
    cum = {}
    dcnt = {}
    sem_of = [None] * n
    val_after = [0] * n
    for k in range(n):
        e, _, _, inc = ops[k]
        if inc == 16:
            c = dcnt.get(e, 0)
            sname = f"{e}_d{c % RDMA}"
            dcnt[e] = c + 1
            v = cum.get(sname, 0) + 16
            cum[sname] = v
            sem_of[k] = sname
            val_after[k] = v
        else:
            sname = e
            c0 = cum.get(sname, 0)
            if need[k]:
                c0 += inc
            cum[sname] = c0
            sem_of[k] = sname
            val_after[k] = c0

    sems = {}
    for e in ("pe", "act", "dve", "pool"):
        t = nc.semaphore(f"sem_{e}")
        ctx.append(t)
        sems[e] = t.__enter__()
    for q in ("sin", "pool"):
        for r in range(RDMA):
            t = nc.semaphore(f"sem_{q}_d{r}")
            ctx.append(t)
            sems[f"{q}_d{r}"] = t.__enter__()

    blk = nc.Block()
    ctx.append(blk)
    block = blk.__enter__()

    def make_section(ename):
        def body(eng):
            waited = {}

            def wge(sname, v):
                if v > waited.get(sname, 0):
                    eng.wait_ge(sems[sname], v)
                    waited[sname] = v

            for k in range(n):
                e, fn, deps, inc = ops[k]
                if e != ename:
                    continue
                for j in sorted(deps):
                    if ops[j][0] == ename and ops[j][3] != 16 and ename not in SELF_SYNC:
                        continue
                    if ops[j][0] == ename and ops[j][3] == 16 and inc == 16:
                        continue
                    wge(sem_of[j], val_after[j])
                if inc == 16:
                    wge(sem_of[k], val_after[k] - 16)
                inst = fn(eng)
                if need[k] and inc != 16:
                    inst.then_inc(sems[ename], inc)
                elif inc == 16:
                    inst.then_inc(sems[sem_of[k]], 16)
            if ename == "pool":
                for r in range(RDMA):
                    sname = f"pool_d{r}"
                    if cum.get(sname, 0) > 0:
                        eng.wait_ge(sems[sname], cum[sname])
        return body

    block.tensor(make_section("pe"))
    block.scalar(make_section("act"))
    block.vector(make_section("dve"))
    block.gpsimd(make_section("pool"))
    block.sync(make_section("sin"))

    for t in reversed(ctx):
        t.__exit__(None, None, None)
    return nc


_NC = None


def kernel(**inputs):
    global _NC
    x = np.asarray(inputs["L0_attn_hidden_states"], dtype=np.float32)
    common = {
        "Wq": np.ascontiguousarray(inputs["L0_attn_Wq"], dtype=np.float32),
        "Wk": np.ascontiguousarray(inputs["L0_attn_Wk"], dtype=np.float32),
        "Wv": np.ascontiguousarray(inputs["L0_attn_Wv"], dtype=np.float32),
        "Wb": np.ascontiguousarray(inputs["L0_attn_Wb"], dtype=np.float32),
        "Wf1": np.ascontiguousarray(inputs["L0_attn_Wf1"], dtype=np.float32),
        "bf1": np.ascontiguousarray(inputs["L0_attn_bf1"], dtype=np.float32),
        "Wf2": np.ascontiguousarray(inputs["L0_attn_Wf2"], dtype=np.float32),
        "bf2": np.ascontiguousarray(inputs["L0_attn_bf2"], dtype=np.float32),
        "Wo": np.ascontiguousarray(inputs["L0_attn_Wo"], dtype=np.float32),
        "rmsw": np.ascontiguousarray(inputs["L0_attn_rms_w"], dtype=np.float32),
        "fq": np.ascontiguousarray(inputs["L0_attn_conv_q_w"], dtype=np.float32),
        "fk": np.ascontiguousarray(inputs["L0_attn_conv_k_w"], dtype=np.float32),
        "fv": np.ascontiguousarray(inputs["L0_attn_conv_v_w"], dtype=np.float32),
        "fs": np.ascontiguousarray(np.asarray(inputs["L0_attn_fir_short_w"]).reshape(H, 7), dtype=np.float32),
        "fl": np.ascontiguousarray(np.asarray(inputs["L0_attn_fir_long_w"]).reshape(H, 64), dtype=np.float32),
        "WfcT": np.ascontiguousarray(np.asarray(inputs["L0_ffn_W_fc"], dtype=np.float32).T),
        "WprojT": np.ascontiguousarray(np.asarray(inputs["L0_ffn_W_proj"], dtype=np.float32).T),
        "ID_F": np.eye(128, dtype=np.float32),
        "ID_B": np.eye(128).astype(BF),
        "NEG_SL": np.ascontiguousarray(-np.tril(np.ones((128, 128), dtype=np.float32), -1)),
        "CAUS_T": np.ascontiguousarray(np.triu(np.ones((128, 128), dtype=np.float32))),
        "ONEROW_F": np.ones((1, 128), dtype=np.float32),
        "ONEROW_B": np.ones((1, 128)).astype(BF),
        "ONECOL_B": np.ones((128, 1)).astype(BF),
        "H4": np.tile(np.eye(4, dtype=np.float32).reshape(1, 16), (128, 1)).astype(BF),
        "IND4R": np.kron(np.eye(4, dtype=np.float32), np.ones((1, 128), dtype=np.float32)).astype(BF),
    }
    if _NC is None:
        _NC = build_nc()
    in_maps = []
    for b in range(8):
        m = dict(common)
        m["xT"] = np.ascontiguousarray(x[b].T)
        in_maps.append(m)
    res = run_bass_kernel_spmd(_NC, in_maps, core_ids=list(range(8)))
    out = np.stack([np.ascontiguousarray(np.asarray(r["outT"]).T) for r in res.results], axis=0)
    return out.astype(np.float32)
